# version 03
# baseline (speedup 1.0000x reference)
import os
import numpy as np
import ml_dtypes
from contextlib import ExitStack

STEP_LIMIT = int(os.environ.get("KSTEPS", "0")) or None

import concourse.bass as bass
import concourse.mybir as mybir
from concourse.bass_utils import run_bass_kernel_spmd

F32 = mybir.dt.float32
BF16 = mybir.dt.bfloat16
I32 = mybir.dt.int32
AF = mybir.ActivationFunctionType
ALU = mybir.AluOpType

NCORES = 8
H = 2048
T = 2048
W = T + 3
WPAD = W + 1
HC = 16
NB = 16
I_FFN = 8192
NIC = 64
EPS = 1e-5
TGX = [(0, 512), (512, 1024), (1024, 1536), (1536, 2048), (2048, 2051)]
TGA = [(3, 515), (515, 1027), (1027, 1539), (1539, 2051)]
TGT = [(0, 512), (512, 1024), (1024, 1536), (1536, 2048)]


class Prog:
    ENGINES = ("pe", "dv", "ac", "pl", "sy")

    def __init__(self):
        self.nsteps = 0
        self.ops = {e: [] for e in self.ENGINES}
        self.ctr = {}
        self.last_write = {}
        self.readers = {}
        self.waited = {e: {} for e in self.ENGINES}
        self.sems = {}
        for e in self.ENGINES:
            self.ctr["c_" + e] = 0
        self.DMASEMS = (["dX", "dS", "dCC", "dDB", "scc"] +
                        [f"dW{i}" for i in range(3)] +
                        [f"dXQ{i}" for i in range(4)] +
                        ["dG0", "dG1", "dU0", "dU1", "dD0", "dD1",
                         "dO0", "dO1"])
        for d in self.DMASEMS:
            self.ctr[d] = 0

    def _wait(self, eng, sem, val, wl):
        if val <= 0:
            return
        if sem == "c_" + eng and eng in ("pe", "sy"):
            return
        if self.waited[eng].get(sem, 0) >= val:
            return
        wl.append((sem, val))

    def step(self, eng, emit, reads=(), writes=(), mile_inc=None,
             record_val=None, tag=""):
        self.nsteps += 1
        if STEP_LIMIT is not None and self.nsteps > STEP_LIMIT:
            return None
        waits = []
        for t in reads:
            lw = self.last_write.get(t)
            if lw is not None:
                self._wait(eng, lw[0], lw[1], waits)
        for t in writes:
            lw = self.last_write.get(t)
            if lw is not None:
                self._wait(eng, lw[0], lw[1], waits)
            for r in self.readers.get(t, ()):
                self._wait(eng, r[0], r[1], waits)
        wd = {}
        for s, v in waits:
            wd[s] = max(wd.get(s, 0), v)
        waits = sorted(wd.items())
        for s, v in waits:
            assert v <= self.ctr[s], (
                f"wait-on-future: {tag} eng={eng} sem={s} v={v} cur={self.ctr[s]}")
            self.waited[eng][s] = max(self.waited[eng].get(s, 0), v)

        if mile_inc is None:
            sem, amt = "c_" + eng, 1
        else:
            sem, amt = mile_inc
        self.ctr[sem] += amt
        mile = (sem, record_val if record_val is not None else self.ctr[sem])
        for t in reads:
            self.readers.setdefault(t, []).append(mile)
        for t in writes:
            self.last_write[t] = mile
            self.readers[t] = [mile]
        self.ops[eng].append((emit, waits, sem, amt, tag))
        return mile

    def raw(self, eng, emit, tag=""):
        self.ops[eng].append((emit, [], None, 0, tag))

    def barrier(self, engines=None):
        engines = engines or list(self.ENGINES)
        targets = [("c_" + e, self.ctr["c_" + e]) for e in engines]
        targets += [(d, self.ctr[d]) for d in self.DMASEMS]
        sems = self.sems
        for e in engines:
            wl = [(s, v) for s, v in targets
                  if s != "c_" + e and v > 0
                  and self.waited[e].get(s, 0) < v]
            for s, v in wl:
                self.waited[e][s] = max(self.waited[e].get(s, 0), v)

            def em2(o, waits, inc_fn, wl=wl):
                for s, v in wl:
                    o.wait_ge(sems[s], v)
            self.ops[e].append((em2, wl, None, 0, "barw"))


def build_nc(debug=False, stop=9):
    nc = bass.Bass("TRN2", target_bir_lowering=False)
    p = Prog()

    xT = nc.dram_tensor("xT", [H, WPAD], F32, kind="ExternalInput").ap()
    WxyT = nc.dram_tensor("WxyT", [H, 2 * H], F32, kind="ExternalInput").ap()
    igw = nc.dram_tensor("igw", [NB * 128, 128], F32, kind="ExternalInput").ap()
    agw = nc.dram_tensor("agw", [NB * 128, 128], F32, kind="ExternalInput").ap()
    igbD = nc.dram_tensor("igb", [128, NB], F32, kind="ExternalInput").ap()
    agbD = nc.dram_tensor("agb", [128, NB], F32, kind="ExternalInput").ap()
    apTD = nc.dram_tensor("apT", [128, NB], F32, kind="ExternalInput").ap()
    cwTD = nc.dram_tensor("cwT", [128, NB * 4], F32, kind="ExternalInput").ap()
    cbTD = nc.dram_tensor("cbT", [128, NB], F32, kind="ExternalInput").ap()
    WrB = nc.dram_tensor("WrB", [HC, HC, 128, 128], F32, kind="ExternalInput").ap()
    WgB = nc.dram_tensor("WgB", [NIC, HC, 128, 128], I32, kind="ExternalInput").ap()
    WuB = nc.dram_tensor("WuB", [NIC, HC, 128, 128], I32, kind="ExternalInput").ap()
    WdB = nc.dram_tensor("WdB", [HC, NIC, 128, 128], I32, kind="ExternalInput").ap()
    gscD = nc.dram_tensor("gscT", [128, NIC], F32, kind="ExternalInput").ap()
    uscD = nc.dram_tensor("uscT", [128, NIC], F32, kind="ExternalInput").ap()
    dscD = nc.dram_tensor("dscT", [128, HC], F32, kind="ExternalInput").ap()
    maskD = nc.dram_tensor("maskin", [1, 8], F32, kind="ExternalInput").ap()

    outT = nc.dram_tensor("outT", [H, T], F32, kind="ExternalOutput").ap()

    cc_src = nc.dram_tensor("cc_src", [2, 128, NB], F32).ap()
    cc_dst = nc.dram_tensor("cc_dst", [NCORES, 2, 128, NB], F32,
                            addr_space="Shared").ap()

    dbg = {}
    if debug:
        for nm, shp in [("d_hs", [H, WPAD]), ("d_z", [H, T]),
                        ("d_A", [128, NB]), ("d_B", [128, NB]),
                        ("d_H", [128, NB]), ("d_x1", [H, T]),
                        ("d_h3", [I_FFN, 512]), ("d_pre", [8, 128, T])]:
            dbg[nm] = nc.dram_tensor(nm, shp, F32, kind="ExternalOutput").ap()

    base = ExitStack()
    E = base.enter_context

    big_a = [E(nc.sbuf_tensor(f"ba{i}", [128, 2056], BF16)) for i in range(HC)]
    big_b = [E(nc.sbuf_tensor(f"bb{i}", [128, 2048], BF16)) for i in range(HC)]
    ones1 = E(nc.sbuf_tensor("ones1", [1, 128], BF16))
    onesK = E(nc.sbuf_tensor("onesK", [128, 1], BF16))
    neg128 = E(nc.sbuf_tensor("neg128", [1, 128], BF16))
    gsc = E(nc.sbuf_tensor("gsc_s", [128, NIC], F32))
    usc = E(nc.sbuf_tensor("usc_s", [128, NIC], F32))
    dsc = E(nc.sbuf_tensor("dsc_s", [128, HC], F32))
    st_m = E(nc.sbuf_tensor("st_m", [1, 512], F32))
    st_q = E(nc.sbuf_tensor("st_q", [1, 512], F32))
    st_v = E(nc.sbuf_tensor("st_v", [1, 512], F32))
    st_v2 = E(nc.sbuf_tensor("st_v2", [1, 512], F32))
    st_rb = E(nc.sbuf_tensor("st_rb", [1, 512], BF16))
    st_mrb = E(nc.sbuf_tensor("st_mrb", [1, 512], BF16))
    bc_r = E(nc.sbuf_tensor("bc_r", [128, 512], BF16))
    bc_mr = E(nc.sbuf_tensor("bc_mr", [128, 512], BF16))
    s3row = E(nc.sbuf_tensor("s3row", [1, 512], BF16))
    s3hrow = E(nc.sbuf_tensor("s3hrow", [1, 512], BF16))
    sq0 = E(nc.sbuf_tensor("sq0", [128, 512], BF16))
    sq1 = E(nc.sbuf_tensor("sq1", [128, 512], BF16))
    epsT = E(nc.sbuf_tensor("epsT", [128, 1], F32))
    hinv = E(nc.sbuf_tensor("hinv", [128, 1], F32))
    negT = E(nc.sbuf_tensor("negT", [128, 1], F32))

    pb = [E(nc.psum_tensor(f"pb{i}", [128, 512], F32)) for i in range(8)]

    for snm in (["c_pe", "c_dv", "c_ac", "c_pl", "c_sy"] + p.DMASEMS):
        p.sems[snm] = E(nc.semaphore(snm))

    SEM = p.sems

    def run_waits(o, waits):
        for s, v in waits:
            o.wait_ge(SEM[s], v)

    def mk(eng):
        def f(name, fn, reads=(), writes=()):
            def em(o, waits, inc_fn):
                run_waits(o, waits)
                inc_fn(fn(o))
            p.step(eng, em, reads, writes, tag=name)
        return f

    dve, act, pol, syq = mk("dv"), mk("ac"), mk("pl"), mk("sy")

    def pe(name, fns, reads=(), writes=()):
        def em(t, waits, inc_fn):
            run_waits(t, waits)
            last = None
            for f in fns:
                last = f(t)
            inc_fn(last)
        p.step("pe", em, reads, writes, tag=name)

    def dma(queue, sem_name, name, fn, reads=(), writes=(), record=None):
        def em(q, waits, inc_fn):
            run_waits(q, waits)
            fn(q).then_inc(SEM[sem_name], 16)
        p.step(queue, em, reads, writes, mile_inc=(sem_name, 16),
               record_val=record, tag=name)

    import os as _os
    _s2b = int(_os.environ.get("S2BANK", "6"))

    def ln_stats_and_bcast(n, src_tiles, src_slices, tagp):
        pe(tagp + "s1",
           [(lambda t, hc=hc: t.matmul(pb[5][0:1, 0:n], onesK[:],
                                       src_slices(hc), start=(hc == 0),
                                       stop=(hc == HC - 1)))
            for hc in range(HC)],
           ["onesK"] + list(src_tiles), ["pb5"])
        for hc in range(HC):
            sqt = [sq0, sq1][hc % 2]
            sqn = ["sq0", "sq1"][hc % 2]
            act(tagp + f"sq{hc}",
                lambda s, hc=hc, sqt=sqt: s.activation(
                    sqt[:, 0:n], src_slices(hc), AF.Square),
                [src_tiles[hc]], [sqn])
            pe(tagp + f"s2_{hc}",
               [lambda t, hc=hc, sqt=sqt: t.matmul(
                   pb[_s2b][0:1, 0:n], onesK[:], sqt[:, 0:n],
                   start=(hc == 0), stop=(hc == HC - 1))],
               ["onesK", sqn], [f"pb{_s2b}"])
        dve(tagp + "m", lambda v: v.tensor_scalar_mul(
            st_m[:, 0:n], pb[5][0:1, 0:n], 1.0 / H), ["pb5"], ["st_m"])
        import os as _os3
        _qm = _os3.environ.get("QMODE", "ts")
        if _qm == "copy":
            dve(tagp + "q", lambda v: v.tensor_copy(
                st_q[:, 0:n], pb[_s2b][0:1, 0:n]),
                [f"pb{_s2b}"], ["st_q"])
        elif _qm == "stv":
            dve(tagp + "q", lambda v: v.tensor_scalar_mul(
                st_v[:, 0:n], pb[_s2b][0:1, 0:n], 1.0 / H),
                [f"pb{_s2b}"], ["st_v"])
        else:
            dve(tagp + "q", lambda v: v.tensor_scalar_mul(
                st_q[:, 0:n], pb[_s2b][0:1, 0:n], 1.0 / H),
                [f"pb{_s2b}"], ["st_q"])
        dve(tagp + "v", lambda v: v.tensor_mul(
            st_v[:, 0:n], st_m[:, 0:n], st_m[:, 0:n]), ["st_m"], ["st_v"])
        dve(tagp + "v2", lambda v: v.tensor_sub(
            st_v2[:, 0:n], st_q[:, 0:n], st_v[:, 0:n]),
            ["st_q", "st_v"], ["st_v2"])
        act(tagp + "sd", lambda s: s.activation(
            st_v[:, 0:n], st_v2[:, 0:n], AF.Sqrt, bias=epsT[0:1, 0:1]),
            ["st_v2", "epsT"], ["st_v"])
        dve(tagp + "r", lambda v: v.reciprocal(st_v2[:, 0:n], st_v[:, 0:n]),
            ["st_v"], ["st_v2"])
        dve(tagp + "rb", lambda v: v.tensor_copy(
            st_rb[:, 0:n], st_v2[:, 0:n]), ["st_v2"], ["st_rb"])
        dve(tagp + "mrb", lambda v: v.tensor_mul(
            st_mrb[:, 0:n], st_m[:, 0:n], st_rb[:, 0:n]),
            ["st_m", "st_rb"], ["st_mrb"])
        pe(tagp + "bc1",
           [lambda t: t.matmul(pb[7][:, 0:n], ones1[:], st_rb[:, 0:n],
                               start=True, stop=True)],
           ["ones1", "st_rb"], ["pb7"])
        dve(tagp + "bcr", lambda v: v.tensor_copy(bc_r[:, 0:n], pb[7][:, 0:n]),
            ["pb7"], ["bc_r"])
        pe(tagp + "bc2",
           [lambda t: t.matmul(pb[7][:, 0:n], ones1[:], st_mrb[:, 0:n],
                               start=True, stop=True)],
           ["ones1", "st_mrb"], ["pb7"])
        dve(tagp + "bcmr", lambda v: v.tensor_copy(
            bc_mr[:, 0:n], pb[7][:, 0:n]), ["pb7"], ["bc_mr"])

    with nc.Block() as block:
        def flush():
            segs = {e: list(p.ops[e]) for e in p.ENGINES}
            for e in p.ENGINES:
                p.ops[e] = []

            def runner(eng_name):
                def run(o):
                    for emit, waits, sem, amt, tag in segs[eng_name]:
                        def inc_fn(instr, sem=sem, amt=amt):
                            if sem is not None:
                                instr.then_inc(SEM[sem], amt)
                        emit(o, waits, inc_fn)
                return run

            block.tensor(runner("pe"))
            block.vector(runner("dv"))
            block.scalar(runner("ac"))
            block.gpsimd(runner("pl"))
            block.sync(runner("sy"))

        with ExitStack() as scA:
            A = scA.enter_context
            igb = A(nc.sbuf_tensor("igb_s", [128, NB], F32))
            agb = A(nc.sbuf_tensor("agb_s", [128, NB], F32))
            apT = A(nc.sbuf_tensor("apT_s", [128, NB], F32))
            csoft = A(nc.sbuf_tensor("csoft", [128, NB], F32))
            cw = A(nc.sbuf_tensor("cw_s", [128, NB * 4], F32))
            cb = A(nc.sbuf_tensor("cb_s", [128, NB], F32))
            mask = A(nc.sbuf_tensor("mask_s", [1, 8], F32))
            maskb = A(nc.sbuf_tensor("mask_bf", [1, 8], BF16))
            maskbc = A(nc.sbuf_tensor("maskbc", [128, 8], F32))
            Asum = A(nc.sbuf_tensor("Asum", [128, NB], F32))
            Aexp = A(nc.sbuf_tensor("Aexp", [128, NB], F32))
            Bcar = A(nc.sbuf_tensor("Bcar", [128, NB], F32))
            ABall = A(nc.sbuf_tensor("ABall", [128, 2 * NB * NCORES], F32))
            Hp = [A(nc.sbuf_tensor(f"Hp{i}", [128, NB], F32))
                  for i in range(2)]
            Hcur = [A(nc.sbuf_tensor(f"Hc{i}", [128, NB], F32))
                    for i in range(2)]
            Htmp = A(nc.sbuf_tensor("Htmp", [128, NB], F32))
            Hprev = A(nc.sbuf_tensor("Hprev", [128, NB], F32))
            igw_b = A(nc.sbuf_tensor("igw_b", [128, NB * 128], BF16))
            agw_b = A(nc.sbuf_tensor("agw_b", [128, NB * 128], BF16))
            slotX = A(nc.sbuf_tensor("slotX", [128, 2056], BF16))
            sc_gx = A(nc.sbuf_tensor("sc_gx", [128, 2048], BF16))
            sc_ga = A(nc.sbuf_tensor("sc_ga", [128, 2048], BF16))
            import os as _os2
            _shrink = bool(_os2.environ.get("SHRINK"))
            u1 = A(nc.sbuf_tensor("u1", [128, 2048], F32))
            u2 = A(nc.sbuf_tensor("u2", [128, 2048], F32))
            u3 = A(nc.sbuf_tensor("u3", [128, 256 if _shrink else 2048], F32))
            wbf = [A(nc.sbuf_tensor(
                f"wbf{i}", [128, 256 if _shrink else 2048], BF16))
                   for i in range(3)]

            def _phases_A():
                import os as _os
                SKIP0 = set(_os.environ.get("SKIP0", ""))
                dve("c_ones1", lambda v: v.memset(ones1[:], 1.0), [], ["ones1"])
                dve("c_onesK", lambda v: v.memset(onesK[:], 1.0), [], ["onesK"])
                dve("c_neg", lambda v: v.memset(neg128[:], -128.0), [], ["neg128"])
                dve("c_hp0", lambda v: v.memset(Hp[0][:], 0.0), [], ["Hp0"])
                dve("c_eps", lambda v: v.memset(epsT[:], EPS), [], ["epsT"])
                dve("c_neg1", lambda v: v.memset(negT[:], -1.0), [], ["negT"])
                dve("c_hinv", lambda v: v.memset(hinv[:], 1.0 / H), [],
                    ["hinv"])

                if "a" not in SKIP0:
                    dma("pl", "dS", "ld_igw",
                        lambda q: q.dma_start(
                            out=igw_b[:].rearrange("p (b j) -> p b j", j=128),
                            in_=igw.rearrange("(b p) j -> p b j", p=128)),
                        [], ["igw_b"], record=176)
                    dma("pl", "dS", "ld_agw",
                        lambda q: q.dma_start(
                            out=agw_b[:].rearrange("p (b j) -> p b j", j=128),
                            in_=agw.rearrange("(b p) j -> p b j", p=128)),
                        [], ["agw_b"], record=176)
                for nm, dst, src in [] if "e" in SKIP0 else [("igb", igb, igbD), ("agb", agb, agbD),
                                     ("apT", apT, apTD), ("cw", cw, cwTD),
                                     ("cb", cb, cbTD), ("gsc", gsc, gscD),
                                     ("usc", usc, uscD), ("dsc", dsc, dscD),
                                     ("mask", mask, maskD)]:
                    dma("sy", "dS", "ld_" + nm,
                        lambda q, d=dst, s=src: q.dma_start(out=d[:], in_=s),
                        [], [nm], record=176)

                if "c" not in SKIP0:
                    _sp()

            def _sp():
                act("sp_abs", lambda s: s.activation(Aexp[:], apT[:], AF.Abs),
                    ["apT"], ["Aexp"])
                act("sp_exp", lambda s: s.activation(Asum[:], Aexp[:], AF.Exp,
                                                     scale=negT[:, 0:1]),
                    ["Aexp", "negT"], ["Asum"])
                act("sp_ln", lambda s: s.activation(Aexp[:], Asum[:], AF.Ln,
                                                    bias=1.0),
                    ["Asum"], ["Aexp"])
                dve("sp_max", lambda v: v.tensor_scalar_max(Asum[:], apT[:], 0.0),
                    ["apT"], ["Asum"])
                dve("csoft", lambda v: v.tensor_add(csoft[:], Asum[:], Aexp[:]),
                    ["Aexp", "Asum"], ["csoft"])

            def _phases_A2():
                import os as _os
                SKIP0 = set(_os.environ.get("SKIP0", ""))
                if "d" not in SKIP0:
                    _maskmm()

            def _maskmm():
                dve("mask_bf", lambda v: v.tensor_copy(maskb[:], mask[:]),
                    ["mask"], ["maskb"])
                pe("mask_mm",
                   [lambda t: t.matmul(pb[7][:, 0:8], ones1[:], maskb[:],
                                       start=True, stop=True)],
                   ["ones1", "maskb"], ["pb7"])
                dve("mask_ev", lambda v: v.tensor_copy(maskbc[:], pb[7][:, 0:8]),
                    ["pb7"], ["maskbc"])

            def _phases_A3():
                if stop < 1:
                    raise StopIteration
                for hc in range(HC):
                    dma("pl", "dX", f"ldx{hc}",
                        lambda q, hc=hc: q.dma_start(
                            out=big_a[hc][:, 0:WPAD],
                            in_=xT[hc * 128:(hc + 1) * 128, :]),
                        [], [f"ba{hc}"], record=256)

                if stop < 1.5:
                    raise StopIteration
                ln1_groups = [(i * 342, (i + 1) * 342) for i in range(6)]
                if stop < 1.7:
                    ln1_groups = ln1_groups[0:1]
                for gi, (c0, c1) in enumerate(ln1_groups):
                    n = c1 - c0
                    names = [f"ba{hc}" for hc in range(HC)]
                    ln_stats_and_bcast(
                        n, names,
                        lambda hc, c0=c0, n=n: big_a[hc][:, c0:c0 + n],
                        f"ln1_{gi}_")
                    for hc in range(HC):
                        sqt = [sq0, sq1][hc % 2]
                        sqn = ["sq0", "sq1"][hc % 2]
                        dve(f"ln1_{gi}_t{hc}",
                            lambda v, hc=hc, c0=c0, n=n, sqt=sqt: v.tensor_mul(
                                sqt[:, 0:n], big_a[hc][:, c0:c0 + n],
                                bc_r[:, 0:n]),
                            [f"ba{hc}", "bc_r"], [sqn])
                        dve(f"ln1_{gi}_a{hc}",
                            lambda v, hc=hc, c0=c0, n=n, sqt=sqt: v.tensor_sub(
                                big_a[hc][:, c0:c0 + n], sqt[:, 0:n],
                                bc_mr[:, 0:n]),
                            [sqn, "bc_mr"], [f"ba{hc}"])

                if debug:
                    for hc in range(HC):
                        dma("pl", "dDB", f"dbg_hs{hc}",
                            lambda q, hc=hc: q.dma_start(
                                out=dbg["d_hs"][hc * 128:(hc + 1) * 128, :],
                                in_=big_a[hc][:, 0:WPAD]), [f"ba{hc}"], [])

                def xy_wdma(b, half, buf):
                    col0 = half * H + b * 128
                    dma("pl", f"dW{buf}", f"wxy{half}_{b}",
                        lambda q: q.dma_start(
                            out=wbf[buf][:].rearrange("p (hc r) -> p hc r", r=128),
                            in_=WxyT[:, col0:col0 + 128].rearrange(
                                "(hc p) r -> p hc r", p=128)),
                        [], [f"wbf{buf}"])

                def emit_block(b, main):
                    xrbuf = b % 2
                    ybuf = 2
                    if b + 1 < NB:
                        xy_wdma(b + 1, 0, (b + 1) % 2)
                    for g, (c0, c1) in enumerate(TGX):
                        n = c1 - c0
                        bank = 2 if g == 4 else g % 2
                        pe(f"xr{b}_{g}",
                           [(lambda t, hc=hc, c0=c0, n=n, bank=bank: t.matmul(
                               pb[bank][:, 0:n],
                               wbf[xrbuf][:, hc * 128:(hc + 1) * 128],
                               big_a[hc][:, c0:c0 + n],
                               start=(hc == 0), stop=(hc == HC - 1)))
                            for hc in range(HC)],
                           [f"wbf{xrbuf}"] + [f"ba{i}" for i in range(HC)],
                           [f"pb{bank}"])
                        dve(f"xre{b}_{g}",
                            lambda v, c0=c0, n=n, bank=bank: v.tensor_copy(
                                slotX[:, c0:c0 + n], pb[bank][:, 0:n]),
                            [f"pb{bank}"], ["slotX"])
                    dve(f"cv{b}_0",
                        lambda v: v.tensor_scalar(
                            u2[:], slotX[:, 0:2048], cw[:, b * 4:b * 4 + 1],
                            cb[:, b:b + 1], ALU.mult, ALU.add),
                        ["slotX", "cw", "cb"], ["u2"])
                    dve(f"cv{b}_1",
                        lambda v: v.scalar_tensor_tensor(
                            u1[:], slotX[:, 1:2049], cw[:, b * 4 + 1:b * 4 + 2],
                            u2[:], ALU.mult, ALU.add),
                        ["slotX", "cw", "u2"], ["u1"])
                    dve(f"cv{b}_2",
                        lambda v: v.scalar_tensor_tensor(
                            u2[:], slotX[:, 2:2050], cw[:, b * 4 + 2:b * 4 + 3],
                            u1[:], ALU.mult, ALU.add),
                        ["slotX", "cw", "u1"], ["u2"])
                    dve(f"cv{b}_3",
                        lambda v: v.scalar_tensor_tensor(
                            u1[:], slotX[:, 3:2051], cw[:, b * 4 + 3:b * 4 + 4],
                            u2[:], ALU.mult, ALU.add),
                        ["slotX", "cw", "u2"], ["u1"])
                    pol(f"xcb{b}", lambda g: g.tensor_copy(slotX[:, 0:2048],
                                                           u1[:]),
                        ["u1"], ["slotX"])
                    for g, (c0, c1) in enumerate(TGT):
                        bx = 3 if g % 2 == 0 else 5
                        ba_ = 4 if g % 2 == 0 else 6
                        pe(f"xg{b}_{g}",
                           [lambda t, c0=c0, c1=c1, bx=bx: t.matmul(
                               pb[bx][:, 0:512],
                               igw_b[:, b * 128:(b + 1) * 128],
                               slotX[:, c0:c1], start=True, stop=True)],
                           ["igw_b", "slotX"], [f"pb{bx}"])
                        act(f"xge{b}_{g}",
                            lambda s, c0=c0, c1=c1, bx=bx: s.activation(
                                sc_gx[:, c0:c1], pb[bx][:, 0:512], AF.Sigmoid,
                                bias=igb[:, b:b + 1]),
                            [f"pb{bx}", "igb"], ["sc_gx"])
                        pe(f"ag{b}_{g}",
                           [lambda t, c0=c0, c1=c1, ba_=ba_: t.matmul(
                               pb[ba_][:, 0:512],
                               agw_b[:, b * 128:(b + 1) * 128],
                               slotX[:, c0:c1], start=True, stop=True)],
                           ["agw_b", "slotX"], [f"pb{ba_}"])
                        act(f"age{b}_{g}",
                            lambda s, c0=c0, c1=c1, ba_=ba_: s.activation(
                                sc_ga[:, c0:c1], pb[ba_][:, 0:512], AF.Sigmoid,
                                bias=agb[:, b:b + 1]),
                            [f"pb{ba_}", "agb"], ["sc_ga"])
                    dve(f"la{b}", lambda v: v.tensor_scalar(
                        u1[:], sc_ga[:], csoft[:, b:b + 1], -8.0,
                        ALU.mult, ALU.mult),
                        ["sc_ga", "csoft"], ["u1"])
                    if not main:
                        dve(f"Ared{b}", lambda v: v.reduce_sum(
                            Asum[:, b:b + 1], u1[:], axis=mybir.AxisListType.X),
                            ["u1"], ["Asum"])
                    act(f"aexp{b}", lambda s: s.activation(u2[:], u1[:], AF.Exp),
                        ["u1"], ["u2"])
                    act(f"asq{b}", lambda s: s.activation(u3[:], u2[:], AF.Square),
                        ["u2"], ["u3"])
                    dve(f"m2{b}", lambda v: v.tensor_scalar(
                        u1[:], u3[:], -1.0, 1.0, ALU.mult, ALU.add),
                        ["u3"], ["u1"])
                    act(f"sqrt{b}", lambda s: s.activation(u3[:], u1[:], AF.Sqrt),
                        ["u1"], ["u3"])
                    dve(f"gm{b}", lambda v: v.tensor_mul(sc_ga[:], sc_gx[:],
                                                         u3[:]),
                        ["sc_gx", "u3"], ["sc_ga"])
                    dve(f"nx{b}", lambda v: v.tensor_mul(
                        u1[:], slotX[:, 0:2048], sc_ga[:]),
                        ["slotX", "sc_ga"], ["u1"])
                    if main:
                        init = Hprev[:, b:b + 1]
                        rd = ["u2", "u1", "Hprev"]
                    else:
                        init = 0.0
                        rd = ["u2", "u1"]
                    dve(f"tts{b}", lambda v, init=init: v.tensor_tensor_scan(
                        u3[:], u2[:], u1[:], init, ALU.mult, ALU.add),
                        rd, ["u3"])
                    if not main:
                        dve(f"bc{b}", lambda v: v.tensor_copy(
                            Bcar[:, b:b + 1], u3[:, 2047:2048]), ["u3"], ["Bcar"])
                    if debug and b == 0 and not main:
                        for slot, (tile, nm) in enumerate(
                                [(slotX, "slotX"), (sc_gx, "sc_gx"),
                                 (sc_ga, "sc_ga"), (u1, "u1"), (u2, "u2"),
                                 (u3, "u3")]):
                            dma("pl", "dDB", f"dp{slot}",
                                lambda q, slot=slot, tile=tile: q.dma_start(
                                    out=dbg["d_pre"][slot],
                                    in_=tile[:, 0:T]),
                                [nm], [])
                        dma("pl", "dDB", "dp6",
                            lambda q: q.dma_start(
                                out=dbg["d_pre"][6][:, 0:NB],
                                in_=csoft[:]), ["csoft"], [])
                        dma("pl", "dDB", "dp7",
                            lambda q: q.dma_start(
                                out=dbg["d_pre"][7][:, 0:NB],
                                in_=Asum[:]), ["Asum"], [])
                        dma("pl", "dDB", "dp8",
                            lambda q: q.dma_start(
                                out=dbg["d_pre"][7][:, 32:32 + NB],
                                in_=apT[:]), ["apT"], [])
                        dma("pl", "dDB", "dp9",
                            lambda q: q.dma_start(
                                out=dbg["d_pre"][7][:, 64:64 + NB],
                                in_=Aexp[:]), ["Aexp"], [])
                    if main:
                        for g, (c0, c1) in enumerate(TGA):
                            bank = g % 2
                            tc0 = c0 - 3
                            pe(f"y{b}_{g}",
                               [(lambda t, hc=hc, c0=c0, bank=bank: t.matmul(
                                   pb[bank][:, 0:512],
                                   wbf[ybuf][:, hc * 128:(hc + 1) * 128],
                                   big_a[hc][:, c0:c0 + 512],
                                   start=(hc == 0), stop=(hc == HC - 1)))
                                for hc in range(HC)],
                               [f"wbf{ybuf}"] + [f"ba{i}" for i in range(HC)],
                               [f"pb{bank}"])
                            act(f"ye{b}_{g}",
                                lambda s, tc0=tc0, bank=bank: s.activation(
                                    slotX[:, tc0:tc0 + 512], pb[bank][:, 0:512],
                                    AF.Gelu),
                                [f"pb{bank}"], ["slotX"])
                        if b + 1 < NB:
                            xy_wdma(b + 1, 1, 2)
                        dve(f"z{b}", lambda v: v.tensor_mul(
                            big_b[b][:], slotX[:, 0:2048], u3[:]),
                            ["slotX", "u3"], [f"bb{b}"])

                if stop >= 2:
                    xy_wdma(0, 0, 0)
                    for b in range(NB):
                        emit_block(b, main=False)

                if stop < 3:
                    raise StopIteration
                act("Aexp", lambda s: s.activation(Aexp[:], Asum[:], AF.Exp),
                    ["Asum"], ["Aexp"])
                dma("sy", "dCC", "ccA",
                    lambda q: q.dma_start(out=cc_src[0], in_=Aexp[:]),
                    ["Aexp"], ["cc_src"])
                dma("sy", "dCC", "ccB",
                    lambda q: q.dma_start(out=cc_src[1], in_=Bcar[:]),
                    ["Bcar"], ["cc_src"])

                def cc_em(g, waits, inc_fn):
                    run_waits(g, waits)
                    g.collective_compute(
                        "AllGather", ALU.bypass,
                        replica_groups=[list(range(NCORES))],
                        ins=[cc_src], outs=[cc_dst],
                    ).then_inc(SEM["scc"], 1)
                p.step("pl", cc_em, ["cc_src"], ["cc_dst"],
                       mile_inc=("scc", 1), tag="cc")

                dma("sy", "dCC", "ccR",
                    lambda q: q.dma_start(
                        out=ABall[:].rearrange("p (j two f) -> p j two f", two=2, f=NB),
                        in_=cc_dst.rearrange("j two p f -> p j two f")),
                    ["cc_dst"], ["ABall"])

                for j in range(NCORES):
                    aoff = j * 2 * NB
                    boff = j * 2 * NB + NB
                    cur = j % 2
                    prv = (j - 1) % 2
                    if j == 0:
                        dve("H0", lambda v, boff=boff: v.tensor_copy(
                            Hcur[0][:], ABall[:, boff:boff + NB]),
                            ["ABall"], ["Hcur0"])
                    else:
                        dve(f"Hm{j}", lambda v, aoff=aoff, prv=prv: v.tensor_mul(
                            Htmp[:], ABall[:, aoff:aoff + NB], Hcur[prv][:]),
                            ["ABall", f"Hcur{prv}"], ["Htmp"])
                        dve(f"Ha{j}", lambda v, boff=boff, cur=cur: v.tensor_add(
                            Hcur[cur][:], Htmp[:], ABall[:, boff:boff + NB]),
                            ["Htmp", "ABall"], [f"Hcur{cur}"])
                    dve(f"Hs{j}", lambda v, j=j, cur=cur: v.scalar_tensor_tensor(
                        Hp[(j + 1) % 2][:], Hcur[cur][:], maskbc[:, j:j + 1],
                        Hp[j % 2][:], ALU.mult, ALU.add),
                        [f"Hcur{cur}", "maskbc", f"Hp{j % 2}"],
                        [f"Hp{(j + 1) % 2}"])
                dve("Hfin", lambda v: v.tensor_copy(Hprev[:], Hp[NCORES % 2][:]),
                    [f"Hp{NCORES % 2}"], ["Hprev"])

                if debug:
                    dma("sy", "dDB", "dbg_A",
                        lambda q: q.dma_start(out=dbg["d_A"], in_=Aexp[:]),
                        ["Aexp"], [])
                    dma("sy", "dDB", "dbg_B",
                        lambda q: q.dma_start(out=dbg["d_B"], in_=Bcar[:]),
                        ["Bcar"], [])
                    dma("sy", "dDB", "dbg_H",
                        lambda q: q.dma_start(out=dbg["d_H"], in_=Hprev[:]),
                        ["Hprev"], [])

                if stop < 4:
                    raise StopIteration
                xy_wdma(0, 0, 0)
                xy_wdma(0, 1, 2)
                for b in range(NB):
                    emit_block(b, main=True)

                if debug:
                    for b in range(NB):
                        dma("pl", "dDB", f"dbg_z{b}",
                            lambda q, b=b: q.dma_start(
                                out=dbg["d_z"][b * 128:(b + 1) * 128, :],
                                in_=big_b[b][:]), [f"bb{b}"], [])
                if stop < 5:
                    raise StopIteration
                p.barrier()

                def wr_dma(hc, buf):
                    dma("pl", f"dW{buf}", f"wr{hc}",
                        lambda q, hc=hc, buf=buf: q.dma_start(
                            out=wbf[buf][:].rearrange("p (rc h) -> p rc h", h=128),
                            in_=WrB[hc].rearrange("rc p h -> p rc h")),
                        [], [f"wbf{buf}"])

                wr_dma(0, 0)
                for hc in range(HC):
                    buf = hc % 2
                    if hc + 1 < HC:
                        wr_dma(hc + 1, (hc + 1) % 2)
                    for g, (c0, c1) in enumerate(TGT):
                        bank = g % 2
                        pe(f"ao{hc}_{g}",
                           [(lambda t, rc=rc, c0=c0, bank=bank, buf=buf: t.matmul(
                               pb[bank][:, 0:512],
                               wbf[buf][:, rc * 128:(rc + 1) * 128],
                               big_b[rc][:, c0:c0 + 512],
                               start=(rc == 0), stop=(rc == HC - 1)))
                            for rc in range(HC)],
                           [f"wbf{buf}"] + [f"bb{i}" for i in range(HC)],
                           [f"pb{bank}"])
                        act(f"aoe{hc}_{g}",
                            lambda s, c0=c0, bank=bank, hc=hc: s.activation(
                                big_a[hc][:, c0:c0 + 512], pb[bank][:, 0:512],
                                AF.Copy),
                            [f"pb{bank}"], [f"ba{hc}"])

                for g, (c0, c1) in enumerate(TGT):
                    ln_stats_and_bcast(
                        512, [f"ba{hc}" for hc in range(HC)],
                        lambda hc, c0=c0: big_a[hc][:, c0:c0 + 512],
                        f"ln2_{g}_")
                    for hc in range(HC):
                        xq = hc % 4
                        xsl = u1[:, xq * 512:(xq + 1) * 512]
                        dma("sy", f"dXQ{xq}", f"x2_{g}_{hc}",
                            lambda q, hc=hc, c0=c0, xsl=xsl: q.dma_start(
                                out=xsl,
                                in_=xT[hc * 128:(hc + 1) * 128,
                                       3 + c0:3 + c0 + 512]),
                            [], [f"u1q{xq}"])
                        sqt, sqn = [(sq0, "sq0"), (sq1, "sq1")][hc % 2]
                        dve(f"ln2_{g}_t{hc}",
                            lambda v, hc=hc, c0=c0, sqt=sqt: v.tensor_mul(
                                sqt[:], big_a[hc][:, c0:c0 + 512], bc_r[:]),
                            [f"ba{hc}", "bc_r"], [sqn])
                        dve(f"ln2_{g}_u{hc}",
                            lambda v, sqt=sqt: v.tensor_sub(
                                sqt[:], sqt[:], bc_mr[:]),
                            [sqn, "bc_mr"], [sqn])
                        dve(f"ln2_{g}_x{hc}",
                            lambda v, hc=hc, c0=c0, sqt=sqt, xsl=xsl:
                            v.tensor_add(
                                big_b[hc][:, c0:c0 + 512], sqt[:], xsl),
                            [sqn, f"u1q{xq}"], [f"bb{hc}"])

                if debug:
                    for b in range(NB):
                        dma("pl", "dDB", f"dbg_x1{b}",
                            lambda q, b=b: q.dma_start(
                                out=dbg["d_x1"][b * 128:(b + 1) * 128, :],
                                in_=big_b[b][:]), [f"bb{b}"], [])


            try:
                _phases_A()
                _phases_A2()
                _phases_A3()
            except StopIteration:
                pass
            p.barrier()
            flush()

        with ExitStack() as scB:
            B = scB.enter_context
            wg_t = [B(nc.sbuf_tensor(f"wg{i}", [128, 2048], BF16))
                    for i in range(2)]
            wu_t = [B(nc.sbuf_tensor(f"wu{i}", [128, 2048], BF16))
                    for i in range(2)]
            wd_t = [B(nc.sbuf_tensor(f"wd{i}", [128, 2048], BF16))
                    for i in range(2)]
            gs_t = [B(nc.sbuf_tensor(f"gs{i}", [128, 512], BF16))
                    for i in range(2)]
            ao4 = [B(nc.sbuf_tensor(f"ao4_{i}", [128, 512], BF16))
                   for i in range(HC)]
            hs3 = [B(nc.sbuf_tensor(f"hs3_{i}", [128, 512], BF16))
                   for i in range(HC)]
            outst = [B(nc.sbuf_tensor(f"outst{i}", [128, 512], F32))
                     for i in range(2)]

            def _phases_B():
                if stop < 6:
                    raise StopIteration

                def ffn_chunk(c):
                    cc0 = c * 512
                    ln_stats_and_bcast(
                        512, [f"bb{hc}" for hc in range(HC)],
                        lambda hc, cc0=cc0: big_b[hc][:, cc0:cc0 + 512],
                        f"ln3_{c}_")
                    for hc in range(HC):
                        sqt, sqn = [(sq0, "sq0"), (sq1, "sq1")][hc % 2]
                        dve(f"ln3_{c}_t{hc}",
                            lambda v, hc=hc, cc0=cc0, sqt=sqt: v.tensor_mul(
                                sqt[:], big_b[hc][:, cc0:cc0 + 512], bc_r[:]),
                            [f"bb{hc}", "bc_r"], [sqn])
                        dve(f"ln3_{c}_a{hc}",
                            lambda v, hc=hc, sqt=sqt: v.tensor_sub(
                                hs3[hc][:], sqt[:], bc_mr[:]),
                            [sqn, "bc_mr"], [f"hs3_{hc}"])
                    pe(f"s3_{c}",
                       [(lambda t, hc=hc: t.matmul(
                           pb[6][0:1, 0:512], onesK[:], hs3[hc][:],
                           start=(hc == 0), stop=(hc == HC - 1)))
                        for hc in range(HC)],
                       ["onesK"] + [f"hs3_{i}" for i in range(HC)], ["pb6"])
                    dve(f"s3e_{c}", lambda v: v.tensor_copy(
                        s3row[:], pb[6][0:1, 0:512]), ["pb6"], ["s3row"])

                    for ic in range(NIC):
                        par = ic % 2
                        dma("pl", f"dG{par}", f"wgd{c}_{ic}",
                            lambda q, ic=ic, par=par: q.dma_start(
                                out=wg_t[par][:].rearrange("p (hc i) -> p hc i", i=128),
                                in_=WgB[ic].rearrange("hc p i -> p hc i")),
                            [], [f"wg{par}"])
                        dma("pl", f"dU{par}", f"wud{c}_{ic}",
                            lambda q, ic=ic, par=par: q.dma_start(
                                out=wu_t[par][:].rearrange("p (hc i) -> p hc i", i=128),
                                in_=WuB[ic].rearrange("hc p i -> p hc i")),
                            [], [f"wu{par}"])
                        bg = [0, 2][par]
                        bu = [1, 3][par]
                        pe(f"g{c}_{ic}",
                           [(lambda t, hc=hc, par=par, bg=bg: t.matmul(
                               pb[bg][:, 0:512],
                               wg_t[par][:, hc * 128:(hc + 1) * 128],
                               hs3[hc][:], start=(hc == 0), stop=False))
                            for hc in range(HC)] +
                           [lambda t, bg=bg: t.matmul(
                               pb[bg][:, 0:512], neg128[:], s3row[:],
                               start=False, stop=True)],
                           [f"wg{par}", "neg128", "s3row"] +
                           [f"hs3_{i}" for i in range(HC)], [f"pb{bg}"])
                        act(f"gse{c}_{ic}",
                            lambda s, ic=ic, par=par, bg=bg: s.activation(
                                gs_t[par][:], pb[bg][:, 0:512], AF.Silu,
                                scale=gsc[:, ic:ic + 1]),
                            [f"pb{bg}", "gsc"], [f"gs{par}"])
                        pe(f"u{c}_{ic}",
                           [(lambda t, hc=hc, par=par, bu=bu: t.matmul(
                               pb[bu][:, 0:512],
                               wu_t[par][:, hc * 128:(hc + 1) * 128],
                               hs3[hc][:], start=(hc == 0), stop=False))
                            for hc in range(HC)] +
                           [lambda t, bu=bu: t.matmul(
                               pb[bu][:, 0:512], neg128[:], s3row[:],
                               start=False, stop=True)],
                           [f"wu{par}", "neg128", "s3row"] +
                           [f"hs3_{i}" for i in range(HC)], [f"pb{bu}"])
                        at, ao_ = ic // 4, (ic % 4) * 512
                        dve(f"h3_{c}_{ic}",
                            lambda v, ic=ic, par=par, bu=bu, at=at, ao_=ao_:
                            v.scalar_tensor_tensor(
                                big_a[at][:, ao_:ao_ + 512], pb[bu][:, 0:512],
                                usc[:, ic:ic + 1], gs_t[par][:],
                                ALU.mult, ALU.mult),
                            [f"pb{bu}", "usc", f"gs{par}"], [f"ba{at}"])

                    if debug and c == 0:
                        for at in range(HC):
                            dma("pl", "dDB", f"dbg_h3{at}",
                                lambda q, at=at: q.dma_start(
                                    out=dbg["d_h3"][at * 512:(at + 1) * 512, :]
                                    .rearrange("(s p) t -> p s t", p=128),
                                    in_=big_a[at][:, 0:2048].rearrange(
                                        "p (s t) -> p s t", t=512)),
                                [f"ba{at}"], [])

                    pe(f"s3h_{c}",
                       [(lambda t, ic=ic: t.matmul(
                           pb[6][0:1, 0:512], onesK[:],
                           big_a[ic // 4][:, (ic % 4) * 512:(ic % 4) * 512 + 512],
                           start=(ic == 0), stop=(ic == NIC - 1)))
                        for ic in range(NIC)],
                       ["onesK"] + [f"ba{i}" for i in range(HC)], ["pb6"])
                    dve(f"s3he_{c}", lambda v: v.tensor_copy(
                        s3hrow[:], pb[6][0:1, 0:512]), ["pb6"], ["s3hrow"])

                    for hc in range(HC):
                        for q4 in range(4):
                            par = q4 % 2
                            dma("pl", f"dD{par}", f"wdd{c}_{hc}_{q4}",
                                lambda q, hc=hc, q4=q4, par=par: q.dma_start(
                                    out=wd_t[par][:].rearrange("p (ic h) -> p ic h", h=128),
                                    in_=WdB[hc, q4 * 16:(q4 + 1) * 16].rearrange(
                                        "ic p h -> p ic h")),
                                [], [f"wd{par}"])
                            fns = []
                            for jj in range(16):
                                j = q4 * 16 + jj
                                at, ao_ = j // 4, (j % 4) * 512
                                fns.append(
                                    lambda t, par=par, jj=jj, at=at, ao_=ao_,
                                    j=j: t.matmul(
                                        pb[4][:, 0:512],
                                        wd_t[par][:, jj * 128:(jj + 1) * 128],
                                        big_a[at][:, ao_:ao_ + 512],
                                        start=(j == 0), stop=False))
                            if q4 == 3:
                                fns.append(lambda t: t.matmul(
                                    pb[4][:, 0:512], neg128[:], s3hrow[:],
                                    start=False, stop=True))
                            pe(f"d{c}_{hc}_{q4}", fns,
                               [f"wd{par}", "neg128", "s3hrow"] +
                               [f"ba{i}" for i in range(HC)], ["pb4"])
                        act(f"de{c}_{hc}",
                            lambda s, hc=hc: s.activation(
                                ao4[hc][:], pb[4][:, 0:512], AF.Copy,
                                scale=dsc[:, hc:hc + 1]),
                            ["pb4", "dsc"], [f"ao4_{hc}"])

                    ln_stats_and_bcast(512, [f"ao4_{hc}" for hc in range(HC)],
                                       lambda hc: ao4[hc][:], f"ln4_{c}_")
                    for hc in range(HC):
                        sqt, sqn = [(sq0, "sq0"), (sq1, "sq1")][hc % 2]
                        ob = hc % 2
                        dve(f"ln4_{c}_t{hc}",
                            lambda v, hc=hc, sqt=sqt: v.tensor_mul(
                                sqt[:], ao4[hc][:], bc_r[:]),
                            [f"ao4_{hc}", "bc_r"], [sqn])
                        dve(f"ln4_{c}_u{hc}",
                            lambda v, sqt=sqt: v.tensor_sub(
                                sqt[:], sqt[:], bc_mr[:]),
                            [sqn, "bc_mr"], [sqn])
                        dve(f"ln4_{c}_o{hc}",
                            lambda v, hc=hc, cc0=cc0, sqt=sqt, ob=ob:
                            v.tensor_add(
                                outst[ob][:], sqt[:],
                                big_b[hc][:, cc0:cc0 + 512]),
                            [sqn, f"bb{hc}"], [f"outst{ob}"])
                        dma("sy", f"dO{ob}", f"out{c}_{hc}",
                            lambda q, hc=hc, cc0=cc0, ob=ob: q.dma_start(
                                out=outT[hc * 128:(hc + 1) * 128,
                                         cc0:cc0 + 512],
                                in_=outst[ob][:]),
                            [f"outst{ob}"], [])

                for c in range(4):
                    ffn_chunk(c)


            try:
                _phases_B()
            except StopIteration:
                pass
            for osem in ("dO0", "dO1"):
                tot = p.ctr[osem]
                if tot:
                    p.raw("sy", lambda o, waits, inc_fn, osem=osem, tot=tot:
                          o.wait_ge(SEM[osem], tot), "fin_" + osem)
            p.barrier()
            flush()

    base.close()
    return nc, dbg



_CACHED = {}


def get_nc(debug=False):
    key = bool(debug)
    if key not in _CACHED:
        _CACHED[key] = build_nc(debug)
    return _CACHED[key]


def make_in_maps(L0_attn_x, L0_attn_W_xy, L0_attn_ig_w, L0_attn_ig_b,
                 L0_attn_ag_w, L0_attn_ag_b, L0_attn_a_param, L0_attn_conv_w,
                 L0_attn_conv_b, L0_attn_W_resid, L0_ffn_gate_w,
                 L0_ffn_gate_scale, L0_ffn_up_w, L0_ffn_up_scale,
                 L0_ffn_down_w, L0_ffn_down_scale):
    C = np.ascontiguousarray

    WxyT = C(np.asarray(L0_attn_W_xy).T)
    igwA = C(np.asarray(L0_attn_ig_w).reshape(NB * 128, 128))
    agwA = C(np.asarray(L0_attn_ag_w).reshape(NB * 128, 128))
    igbA = C(np.asarray(L0_attn_ig_b).T)
    agbA = C(np.asarray(L0_attn_ag_b).T)
    apTA = C(np.asarray(L0_attn_a_param).reshape(NB, 128).T)
    cwTA = C(np.asarray(L0_attn_conv_w).T.reshape(NB, 128, 4)
             .transpose(1, 0, 2).reshape(128, NB * 4))
    cbTA = C(np.asarray(L0_attn_conv_b).reshape(NB, 128).T)
    Wr = np.asarray(L0_attn_W_resid)
    WrBA = C(Wr.T.reshape(HC, 128, HC, 128).transpose(2, 0, 1, 3))
    gw = np.asarray(L0_ffn_gate_w)
    WgBA = C(gw.T.reshape(HC, 128, NIC, 128).transpose(2, 0, 1, 3))
    uw = np.asarray(L0_ffn_up_w)
    WuBA = C(uw.T.reshape(HC, 128, NIC, 128).transpose(2, 0, 1, 3))
    dw = np.asarray(L0_ffn_down_w)
    WdBA = C(dw.T.reshape(NIC, 128, HC, 128).transpose(2, 0, 1, 3))
    gscA = C(np.asarray(L0_ffn_gate_scale).reshape(NIC, 128).T)
    uscA = C(np.asarray(L0_ffn_up_scale).reshape(NIC, 128).T)
    dscA = C(np.asarray(L0_ffn_down_scale).reshape(HC, 128).T)

    xf = np.asarray(L0_attn_x).reshape(16384, H).astype(np.float32)
    in_maps = []
    for c in range(NCORES):
        lo = c * T
        sl = np.zeros((WPAD, H), np.float32)
        if c > 0:
            sl[0:3] = xf[lo - 3:lo]
        sl[3:3 + T] = xf[lo:lo + T]
        xTc = C(sl.T)
        m = np.zeros((1, 8), np.float32)
        if c > 0:
            m[0, c - 1] = 1.0
        in_maps.append(dict(
            xT=xTc, WxyT=WxyT, igw=igwA, agw=agwA, igb=igbA, agb=agbA,
            apT=apTA, cwT=cwTA, cbT=cbTA, WrB=WrBA, WgB=WgBA, WuB=WuBA,
            WdB=WdBA, gscT=gscA, uscT=uscA, dscT=dscA, maskin=m))
    return in_maps


def kernel(**inputs):
    in_maps = make_in_maps(**inputs)
    nc, _ = get_nc(debug=False)
    res = run_bass_kernel_spmd(nc, in_maps, core_ids=list(range(NCORES)))
    outs = [np.asarray(r["outT"]).T for r in res.results]
    return np.concatenate(outs, axis=0).reshape(1, 16384, H).astype(np.float32)


# version 04
# speedup vs baseline: 1.0181x; 1.0181x over previous
import os
import numpy as np
import ml_dtypes
from contextlib import ExitStack

STEP_LIMIT = int(os.environ.get("KSTEPS", "0")) or None

import concourse.bass as bass
import concourse.mybir as mybir
from concourse.bass_utils import run_bass_kernel_spmd

F32 = mybir.dt.float32
BF16 = mybir.dt.bfloat16
I32 = mybir.dt.int32
AF = mybir.ActivationFunctionType
ALU = mybir.AluOpType

NCORES = 8
H = 2048
T = 2048
W = T + 3
WPAD = W + 1
HC = 16
NB = 16
I_FFN = 8192
NIC = 64
EPS = 1e-5
TGX = [(0, 512), (512, 1024), (1024, 1536), (1536, 2048), (2048, 2051)]
TGA = [(3, 515), (515, 1027), (1027, 1539), (1539, 2051)]
TGT = [(0, 512), (512, 1024), (1024, 1536), (1536, 2048)]


class Prog:
    ENGINES = ("pe", "dv", "ac", "pl", "sy")

    def __init__(self):
        self.nsteps = 0
        self.ops = {e: [] for e in self.ENGINES}
        self.ctr = {}
        self.last_write = {}
        self.readers = {}
        self.waited = {e: {} for e in self.ENGINES}
        self.sems = {}
        for e in self.ENGINES:
            self.ctr["c_" + e] = 0
        self.DMASEMS = (["dX", "dS", "dCC", "dDB", "scc"] +
                        [f"dW{i}" for i in range(3)] +
                        [f"dXQ{i}" for i in range(4)] +
                        ["dG0", "dG1", "dU0", "dU1", "dD0", "dD1",
                         "dO0", "dO1", "dCB"])
        for d in self.DMASEMS:
            self.ctr[d] = 0

    def _wait(self, eng, sem, val, wl):
        if val <= 0:
            return
        if sem == "c_" + eng and eng in ("pe", "sy"):
            return
        if self.waited[eng].get(sem, 0) >= val:
            return
        wl.append((sem, val))

    def step(self, eng, emit, reads=(), writes=(), mile_inc=None,
             record_val=None, tag=""):
        self.nsteps += 1
        if STEP_LIMIT is not None and self.nsteps > STEP_LIMIT:
            return None
        waits = []
        for t in reads:
            lw = self.last_write.get(t)
            if lw is not None:
                self._wait(eng, lw[0], lw[1], waits)
        for t in writes:
            lw = self.last_write.get(t)
            if lw is not None:
                self._wait(eng, lw[0], lw[1], waits)
            for r in self.readers.get(t, ()):
                self._wait(eng, r[0], r[1], waits)
        wd = {}
        for s, v in waits:
            wd[s] = max(wd.get(s, 0), v)
        waits = sorted(wd.items())
        for s, v in waits:
            assert v <= self.ctr[s], (
                f"wait-on-future: {tag} eng={eng} sem={s} v={v} cur={self.ctr[s]}")
            self.waited[eng][s] = max(self.waited[eng].get(s, 0), v)

        if mile_inc is None:
            sem, amt = "c_" + eng, 1
        else:
            sem, amt = mile_inc
        self.ctr[sem] += amt
        mile = (sem, record_val if record_val is not None else self.ctr[sem])
        for t in reads:
            self.readers.setdefault(t, []).append(mile)
        for t in writes:
            self.last_write[t] = mile
            self.readers[t] = [mile]
        self.ops[eng].append((emit, waits, sem, amt, tag))
        return mile

    def raw(self, eng, emit, tag=""):
        self.ops[eng].append((emit, [], None, 0, tag))

    def barrier(self, engines=None):
        engines = engines or list(self.ENGINES)
        targets = [("c_" + e, self.ctr["c_" + e]) for e in engines]
        targets += [(d, self.ctr[d]) for d in self.DMASEMS]
        sems = self.sems
        for e in engines:
            wl = [(s, v) for s, v in targets
                  if s != "c_" + e and v > 0
                  and self.waited[e].get(s, 0) < v]
            for s, v in wl:
                self.waited[e][s] = max(self.waited[e].get(s, 0), v)

            def em2(o, waits, inc_fn, wl=wl):
                for s, v in wl:
                    o.wait_ge(sems[s], v)
            self.ops[e].append((em2, wl, None, 0, "barw"))


def build_nc(debug=False, stop=9):
    nc = bass.Bass("TRN2", target_bir_lowering=False)
    p = Prog()

    xT = nc.dram_tensor("xT", [H, WPAD], F32, kind="ExternalInput").ap()
    WxyT = nc.dram_tensor("WxyT", [H, 2 * H], F32, kind="ExternalInput").ap()
    igw = nc.dram_tensor("igw", [NB * 128, 128], F32, kind="ExternalInput").ap()
    agw = nc.dram_tensor("agw", [NB * 128, 128], F32, kind="ExternalInput").ap()
    igbD = nc.dram_tensor("igb", [128, NB], F32, kind="ExternalInput").ap()
    agbD = nc.dram_tensor("agb", [128, NB], F32, kind="ExternalInput").ap()
    apTD = nc.dram_tensor("apT", [128, NB], F32, kind="ExternalInput").ap()
    cwTD = nc.dram_tensor("cwT", [128, NB * 4], F32, kind="ExternalInput").ap()
    cbTD = nc.dram_tensor("cbT", [128, NB], F32, kind="ExternalInput").ap()
    WrB = nc.dram_tensor("WrB", [HC, HC, 128, 128], F32, kind="ExternalInput").ap()
    WgB = nc.dram_tensor("WgB", [NIC, HC, 128, 128], I32, kind="ExternalInput").ap()
    WuB = nc.dram_tensor("WuB", [NIC, HC, 128, 128], I32, kind="ExternalInput").ap()
    WdB = nc.dram_tensor("WdB", [HC, NIC, 128, 128], I32, kind="ExternalInput").ap()
    gscD = nc.dram_tensor("gscT", [128, NIC], F32, kind="ExternalInput").ap()
    uscD = nc.dram_tensor("uscT", [128, NIC], F32, kind="ExternalInput").ap()
    dscD = nc.dram_tensor("dscT", [128, HC], F32, kind="ExternalInput").ap()
    maskD = nc.dram_tensor("maskin", [1, 8], F32, kind="ExternalInput").ap()

    outT = nc.dram_tensor("outT", [H, T], F32, kind="ExternalOutput").ap()

    WgC = nc.dram_tensor("WgC", [NIC, HC, 128, 128], BF16).ap()
    WuC = nc.dram_tensor("WuC", [NIC, HC, 128, 128], BF16).ap()
    WdC = nc.dram_tensor("WdC", [HC, NIC, 128, 128], BF16).ap()
    cc_src = nc.dram_tensor("cc_src", [2, 128, NB], F32).ap()
    cc_dst = nc.dram_tensor("cc_dst", [NCORES, 2, 128, NB], F32,
                            addr_space="Shared").ap()

    dbg = {}
    if debug:
        for nm, shp in [("d_hs", [H, WPAD]), ("d_z", [H, T]),
                        ("d_A", [128, NB]), ("d_B", [128, NB]),
                        ("d_H", [128, NB]), ("d_x1", [H, T]),
                        ("d_h3", [I_FFN, 512]), ("d_pre", [8, 128, T])]:
            dbg[nm] = nc.dram_tensor(nm, shp, F32, kind="ExternalOutput").ap()

    base = ExitStack()
    E = base.enter_context

    big_a = [E(nc.sbuf_tensor(f"ba{i}", [128, 2056], BF16)) for i in range(HC)]
    big_b = [E(nc.sbuf_tensor(f"bb{i}", [128, 2048], BF16)) for i in range(HC)]
    ones1 = E(nc.sbuf_tensor("ones1", [1, 128], BF16))
    onesK = E(nc.sbuf_tensor("onesK", [128, 1], BF16))
    neg128 = E(nc.sbuf_tensor("neg128", [1, 128], BF16))
    gsc = E(nc.sbuf_tensor("gsc_s", [128, NIC], F32))
    usc = E(nc.sbuf_tensor("usc_s", [128, NIC], F32))
    dsc = E(nc.sbuf_tensor("dsc_s", [128, HC], F32))
    st_m = E(nc.sbuf_tensor("st_m", [1, 512], F32))
    st_q = E(nc.sbuf_tensor("st_q", [1, 512], F32))
    st_v = E(nc.sbuf_tensor("st_v", [1, 512], F32))
    st_v2 = E(nc.sbuf_tensor("st_v2", [1, 512], F32))
    st_rb = E(nc.sbuf_tensor("st_rb", [1, 512], BF16))
    st_mrb = E(nc.sbuf_tensor("st_mrb", [1, 512], BF16))
    bc_r = E(nc.sbuf_tensor("bc_r", [128, 512], BF16))
    bc_mr = E(nc.sbuf_tensor("bc_mr", [128, 512], BF16))
    s3row = E(nc.sbuf_tensor("s3row", [1, 512], BF16))
    s3hrow = E(nc.sbuf_tensor("s3hrow", [1, 512], BF16))
    sq0 = E(nc.sbuf_tensor("sq0", [128, 512], BF16))
    sq1 = E(nc.sbuf_tensor("sq1", [128, 512], BF16))
    epsT = E(nc.sbuf_tensor("epsT", [128, 1], F32))
    hinv = E(nc.sbuf_tensor("hinv", [128, 1], F32))
    negT = E(nc.sbuf_tensor("negT", [128, 1], F32))

    pb = [E(nc.psum_tensor(f"pb{i}", [128, 512], F32)) for i in range(8)]

    for snm in (["c_pe", "c_dv", "c_ac", "c_pl", "c_sy"] + p.DMASEMS):
        p.sems[snm] = E(nc.semaphore(snm))

    SEM = p.sems

    def run_waits(o, waits):
        for s, v in waits:
            o.wait_ge(SEM[s], v)

    def mk(eng):
        def f(name, fn, reads=(), writes=()):
            def em(o, waits, inc_fn):
                run_waits(o, waits)
                inc_fn(fn(o))
            p.step(eng, em, reads, writes, tag=name)
        return f

    dve, act, pol, syq = mk("dv"), mk("ac"), mk("pl"), mk("sy")

    def pe(name, fns, reads=(), writes=()):
        def em(t, waits, inc_fn):
            run_waits(t, waits)
            last = None
            for f in fns:
                last = f(t)
            inc_fn(last)
        p.step("pe", em, reads, writes, tag=name)

    def dma(queue, sem_name, name, fn, reads=(), writes=(), record=None):
        def em(q, waits, inc_fn):
            run_waits(q, waits)
            fn(q).then_inc(SEM[sem_name], 16)
        p.step(queue, em, reads, writes, mile_inc=(sem_name, 16),
               record_val=record, tag=name)

    import os as _os
    _s2b = int(_os.environ.get("S2BANK", "6"))

    def ln_stats_and_bcast(n, src_tiles, src_slices, tagp):
        pe(tagp + "s1",
           [(lambda t, hc=hc: t.matmul(pb[5][0:1, 0:n], onesK[:],
                                       src_slices(hc), start=(hc == 0),
                                       stop=(hc == HC - 1)))
            for hc in range(HC)],
           ["onesK"] + list(src_tiles), ["pb5"])
        for hc in range(HC):
            sqt = [sq0, sq1][hc % 2]
            sqn = ["sq0", "sq1"][hc % 2]
            act(tagp + f"sq{hc}",
                lambda s, hc=hc, sqt=sqt: s.activation(
                    sqt[:, 0:n], src_slices(hc), AF.Square),
                [src_tiles[hc]], [sqn])
            pe(tagp + f"s2_{hc}",
               [lambda t, hc=hc, sqt=sqt: t.matmul(
                   pb[_s2b][0:1, 0:n], onesK[:], sqt[:, 0:n],
                   start=(hc == 0), stop=(hc == HC - 1))],
               ["onesK", sqn], [f"pb{_s2b}"])
        dve(tagp + "m", lambda v: v.tensor_scalar_mul(
            st_m[:, 0:n], pb[5][0:1, 0:n], 1.0 / H), ["pb5"], ["st_m"])
        import os as _os3
        _qm = _os3.environ.get("QMODE", "ts")
        if _qm == "copy":
            dve(tagp + "q", lambda v: v.tensor_copy(
                st_q[:, 0:n], pb[_s2b][0:1, 0:n]),
                [f"pb{_s2b}"], ["st_q"])
        elif _qm == "stv":
            dve(tagp + "q", lambda v: v.tensor_scalar_mul(
                st_v[:, 0:n], pb[_s2b][0:1, 0:n], 1.0 / H),
                [f"pb{_s2b}"], ["st_v"])
        else:
            dve(tagp + "q", lambda v: v.tensor_scalar_mul(
                st_q[:, 0:n], pb[_s2b][0:1, 0:n], 1.0 / H),
                [f"pb{_s2b}"], ["st_q"])
        dve(tagp + "v", lambda v: v.tensor_mul(
            st_v[:, 0:n], st_m[:, 0:n], st_m[:, 0:n]), ["st_m"], ["st_v"])
        dve(tagp + "v2", lambda v: v.tensor_sub(
            st_v2[:, 0:n], st_q[:, 0:n], st_v[:, 0:n]),
            ["st_q", "st_v"], ["st_v2"])
        act(tagp + "sd", lambda s: s.activation(
            st_v[:, 0:n], st_v2[:, 0:n], AF.Sqrt, bias=epsT[0:1, 0:1]),
            ["st_v2", "epsT"], ["st_v"])
        dve(tagp + "r", lambda v: v.reciprocal(st_v2[:, 0:n], st_v[:, 0:n]),
            ["st_v"], ["st_v2"])
        dve(tagp + "rb", lambda v: v.tensor_copy(
            st_rb[:, 0:n], st_v2[:, 0:n]), ["st_v2"], ["st_rb"])
        dve(tagp + "mrb", lambda v: v.tensor_mul(
            st_mrb[:, 0:n], st_m[:, 0:n], st_rb[:, 0:n]),
            ["st_m", "st_rb"], ["st_mrb"])
        pe(tagp + "bc1",
           [lambda t: t.matmul(pb[7][:, 0:n], ones1[:], st_rb[:, 0:n],
                               start=True, stop=True)],
           ["ones1", "st_rb"], ["pb7"])
        dve(tagp + "bcr", lambda v: v.tensor_copy(bc_r[:, 0:n], pb[7][:, 0:n]),
            ["pb7"], ["bc_r"])
        pe(tagp + "bc2",
           [lambda t: t.matmul(pb[7][:, 0:n], ones1[:], st_mrb[:, 0:n],
                               start=True, stop=True)],
           ["ones1", "st_mrb"], ["pb7"])
        dve(tagp + "bcmr", lambda v: v.tensor_copy(
            bc_mr[:, 0:n], pb[7][:, 0:n]), ["pb7"], ["bc_mr"])

    with nc.Block() as block:
        def flush():
            segs = {e: list(p.ops[e]) for e in p.ENGINES}
            for e in p.ENGINES:
                p.ops[e] = []

            def runner(eng_name):
                def run(o):
                    for emit, waits, sem, amt, tag in segs[eng_name]:
                        def inc_fn(instr, sem=sem, amt=amt):
                            if sem is not None:
                                instr.then_inc(SEM[sem], amt)
                        emit(o, waits, inc_fn)
                return run

            block.tensor(runner("pe"))
            block.vector(runner("dv"))
            block.scalar(runner("ac"))
            block.gpsimd(runner("pl"))
            block.sync(runner("sy"))

        with ExitStack() as scA:
            A = scA.enter_context
            igb = A(nc.sbuf_tensor("igb_s", [128, NB], F32))
            agb = A(nc.sbuf_tensor("agb_s", [128, NB], F32))
            apT = A(nc.sbuf_tensor("apT_s", [128, NB], F32))
            csoft = A(nc.sbuf_tensor("csoft", [128, NB], F32))
            cw = A(nc.sbuf_tensor("cw_s", [128, NB * 4], F32))
            cb = A(nc.sbuf_tensor("cb_s", [128, NB], F32))
            mask = A(nc.sbuf_tensor("mask_s", [1, 8], F32))
            maskb = A(nc.sbuf_tensor("mask_bf", [1, 8], BF16))
            maskbc = A(nc.sbuf_tensor("maskbc", [128, 8], F32))
            Asum = A(nc.sbuf_tensor("Asum", [128, NB], F32))
            Aexp = A(nc.sbuf_tensor("Aexp", [128, NB], F32))
            Bcar = A(nc.sbuf_tensor("Bcar", [128, NB], F32))
            ABall = A(nc.sbuf_tensor("ABall", [128, 2 * NB * NCORES], F32))
            Hp = [A(nc.sbuf_tensor(f"Hp{i}", [128, NB], F32))
                  for i in range(2)]
            Hcur = [A(nc.sbuf_tensor(f"Hc{i}", [128, NB], F32))
                    for i in range(2)]
            Htmp = A(nc.sbuf_tensor("Htmp", [128, NB], F32))
            Hprev = A(nc.sbuf_tensor("Hprev", [128, NB], F32))
            igw_b = A(nc.sbuf_tensor("igw_b", [128, NB * 128], BF16))
            agw_b = A(nc.sbuf_tensor("agw_b", [128, NB * 128], BF16))
            slotX = A(nc.sbuf_tensor("slotX", [128, 2056], BF16))
            sc_gx = A(nc.sbuf_tensor("sc_gx", [128, 2048], BF16))
            sc_ga = A(nc.sbuf_tensor("sc_ga", [128, 2048], BF16))
            import os as _os2
            _shrink = bool(_os2.environ.get("SHRINK"))
            u1 = A(nc.sbuf_tensor("u1", [128, 2048], F32))
            u2 = A(nc.sbuf_tensor("u2", [128, 2048], F32))
            u3 = A(nc.sbuf_tensor("u3", [128, 256 if _shrink else 2048], F32))
            wbf = [A(nc.sbuf_tensor(
                f"wbf{i}", [128, 256 if _shrink else 2048], BF16))
                   for i in range(3)]

            def _phases_A():
                import os as _os
                SKIP0 = set(_os.environ.get("SKIP0", ""))
                dve("c_ones1", lambda v: v.memset(ones1[:], 1.0), [], ["ones1"])
                dve("c_onesK", lambda v: v.memset(onesK[:], 1.0), [], ["onesK"])
                dve("c_neg", lambda v: v.memset(neg128[:], -128.0), [], ["neg128"])
                dve("c_hp0", lambda v: v.memset(Hp[0][:], 0.0), [], ["Hp0"])
                dve("c_eps", lambda v: v.memset(epsT[:], EPS), [], ["epsT"])
                dve("c_neg1", lambda v: v.memset(negT[:], -1.0), [], ["negT"])
                dve("c_hinv", lambda v: v.memset(hinv[:], 1.0 / H), [],
                    ["hinv"])

                if "a" not in SKIP0:
                    dma("pl", "dS", "ld_igw",
                        lambda q: q.dma_start(
                            out=igw_b[:].rearrange("p (b j) -> p b j", j=128),
                            in_=igw.rearrange("(b p) j -> p b j", p=128)),
                        [], ["igw_b"], record=176)
                    dma("pl", "dS", "ld_agw",
                        lambda q: q.dma_start(
                            out=agw_b[:].rearrange("p (b j) -> p b j", j=128),
                            in_=agw.rearrange("(b p) j -> p b j", p=128)),
                        [], ["agw_b"], record=176)
                for nm, dst, src in [] if "e" in SKIP0 else [("igb", igb, igbD), ("agb", agb, agbD),
                                     ("apT", apT, apTD), ("cw", cw, cwTD),
                                     ("cb", cb, cbTD), ("gsc", gsc, gscD),
                                     ("usc", usc, uscD), ("dsc", dsc, dscD),
                                     ("mask", mask, maskD)]:
                    dma("sy", "dS", "ld_" + nm,
                        lambda q, d=dst, s=src: q.dma_start(out=d[:], in_=s),
                        [], [nm], record=176)

                if "c" not in SKIP0:
                    _sp()

            def _sp():
                act("sp_abs", lambda s: s.activation(Aexp[:], apT[:], AF.Abs),
                    ["apT"], ["Aexp"])
                act("sp_exp", lambda s: s.activation(Asum[:], Aexp[:], AF.Exp,
                                                     scale=negT[:, 0:1]),
                    ["Aexp", "negT"], ["Asum"])
                act("sp_ln", lambda s: s.activation(Aexp[:], Asum[:], AF.Ln,
                                                    bias=1.0),
                    ["Asum"], ["Aexp"])
                dve("sp_max", lambda v: v.tensor_scalar_max(Asum[:], apT[:], 0.0),
                    ["apT"], ["Asum"])
                dve("csoft", lambda v: v.tensor_add(csoft[:], Asum[:], Aexp[:]),
                    ["Aexp", "Asum"], ["csoft"])

            def _phases_A2():
                import os as _os
                SKIP0 = set(_os.environ.get("SKIP0", ""))
                if "d" not in SKIP0:
                    _maskmm()

            def _maskmm():
                dve("mask_bf", lambda v: v.tensor_copy(maskb[:], mask[:]),
                    ["mask"], ["maskb"])
                pe("mask_mm",
                   [lambda t: t.matmul(pb[7][:, 0:8], ones1[:], maskb[:],
                                       start=True, stop=True)],
                   ["ones1", "maskb"], ["pb7"])
                dve("mask_ev", lambda v: v.tensor_copy(maskbc[:], pb[7][:, 0:8]),
                    ["pb7"], ["maskbc"])

            def _phases_A3():
                if stop < 1:
                    raise StopIteration
                for hc in range(HC):
                    dma("pl", "dX", f"ldx{hc}",
                        lambda q, hc=hc: q.dma_start(
                            out=big_a[hc][:, 0:WPAD],
                            in_=xT[hc * 128:(hc + 1) * 128, :]),
                        [], [f"ba{hc}"], record=256)

                if stop < 1.5:
                    raise StopIteration
                ln1_groups = [(i * 342, (i + 1) * 342) for i in range(6)]
                if stop < 1.7:
                    ln1_groups = ln1_groups[0:1]
                for gi, (c0, c1) in enumerate(ln1_groups):
                    n = c1 - c0
                    names = [f"ba{hc}" for hc in range(HC)]
                    ln_stats_and_bcast(
                        n, names,
                        lambda hc, c0=c0, n=n: big_a[hc][:, c0:c0 + n],
                        f"ln1_{gi}_")
                    for hc in range(HC):
                        sqt = [sq0, sq1][hc % 2]
                        sqn = ["sq0", "sq1"][hc % 2]
                        dve(f"ln1_{gi}_t{hc}",
                            lambda v, hc=hc, c0=c0, n=n, sqt=sqt: v.tensor_mul(
                                sqt[:, 0:n], big_a[hc][:, c0:c0 + n],
                                bc_r[:, 0:n]),
                            [f"ba{hc}", "bc_r"], [sqn])
                        dve(f"ln1_{gi}_a{hc}",
                            lambda v, hc=hc, c0=c0, n=n, sqt=sqt: v.tensor_sub(
                                big_a[hc][:, c0:c0 + n], sqt[:, 0:n],
                                bc_mr[:, 0:n]),
                            [sqn, "bc_mr"], [f"ba{hc}"])

                if debug:
                    for hc in range(HC):
                        dma("pl", "dDB", f"dbg_hs{hc}",
                            lambda q, hc=hc: q.dma_start(
                                out=dbg["d_hs"][hc * 128:(hc + 1) * 128, :],
                                in_=big_a[hc][:, 0:WPAD]), [f"ba{hc}"], [])

                def xy_wdma(b, half, buf):
                    col0 = half * H + b * 128
                    dma("pl", f"dW{buf}", f"wxy{half}_{b}",
                        lambda q: q.dma_start(
                            out=wbf[buf][:].rearrange("p (hc r) -> p hc r", r=128),
                            in_=WxyT[:, col0:col0 + 128].rearrange(
                                "(hc p) r -> p hc r", p=128)),
                        [], [f"wbf{buf}"])

                def emit_block(b, main):
                    xrbuf = b % 2
                    ybuf = 2
                    if b + 1 < NB:
                        xy_wdma(b + 1, 0, (b + 1) % 2)
                    for g, (c0, c1) in enumerate(TGX):
                        n = c1 - c0
                        bank = 2 if g == 4 else g % 2
                        pe(f"xr{b}_{g}",
                           [(lambda t, hc=hc, c0=c0, n=n, bank=bank: t.matmul(
                               pb[bank][:, 0:n],
                               wbf[xrbuf][:, hc * 128:(hc + 1) * 128],
                               big_a[hc][:, c0:c0 + n],
                               start=(hc == 0), stop=(hc == HC - 1)))
                            for hc in range(HC)],
                           [f"wbf{xrbuf}"] + [f"ba{i}" for i in range(HC)],
                           [f"pb{bank}"])
                        dve(f"xre{b}_{g}",
                            lambda v, c0=c0, n=n, bank=bank: v.tensor_copy(
                                slotX[:, c0:c0 + n], pb[bank][:, 0:n]),
                            [f"pb{bank}"], ["slotX"])
                    dve(f"cv{b}_0",
                        lambda v: v.tensor_scalar(
                            u2[:], slotX[:, 0:2048], cw[:, b * 4:b * 4 + 1],
                            cb[:, b:b + 1], ALU.mult, ALU.add),
                        ["slotX", "cw", "cb"], ["u2"])
                    dve(f"cv{b}_1",
                        lambda v: v.scalar_tensor_tensor(
                            u1[:], slotX[:, 1:2049], cw[:, b * 4 + 1:b * 4 + 2],
                            u2[:], ALU.mult, ALU.add),
                        ["slotX", "cw", "u2"], ["u1"])
                    dve(f"cv{b}_2",
                        lambda v: v.scalar_tensor_tensor(
                            u2[:], slotX[:, 2:2050], cw[:, b * 4 + 2:b * 4 + 3],
                            u1[:], ALU.mult, ALU.add),
                        ["slotX", "cw", "u1"], ["u2"])
                    dve(f"cv{b}_3",
                        lambda v: v.scalar_tensor_tensor(
                            u1[:], slotX[:, 3:2051], cw[:, b * 4 + 3:b * 4 + 4],
                            u2[:], ALU.mult, ALU.add),
                        ["slotX", "cw", "u2"], ["u1"])
                    pol(f"xcb{b}", lambda g: g.tensor_copy(slotX[:, 0:2048],
                                                           u1[:]),
                        ["u1"], ["slotX"])
                    for g, (c0, c1) in enumerate(TGT):
                        bx = 3 if g % 2 == 0 else 5
                        ba_ = 4 if g % 2 == 0 else 6
                        pe(f"xg{b}_{g}",
                           [lambda t, c0=c0, c1=c1, bx=bx: t.matmul(
                               pb[bx][:, 0:512],
                               igw_b[:, b * 128:(b + 1) * 128],
                               slotX[:, c0:c1], start=True, stop=True)],
                           ["igw_b", "slotX"], [f"pb{bx}"])
                        act(f"xge{b}_{g}",
                            lambda s, c0=c0, c1=c1, bx=bx: s.activation(
                                sc_gx[:, c0:c1], pb[bx][:, 0:512], AF.Sigmoid,
                                bias=igb[:, b:b + 1]),
                            [f"pb{bx}", "igb"], ["sc_gx"])
                        pe(f"ag{b}_{g}",
                           [lambda t, c0=c0, c1=c1, ba_=ba_: t.matmul(
                               pb[ba_][:, 0:512],
                               agw_b[:, b * 128:(b + 1) * 128],
                               slotX[:, c0:c1], start=True, stop=True)],
                           ["agw_b", "slotX"], [f"pb{ba_}"])
                        act(f"age{b}_{g}",
                            lambda s, c0=c0, c1=c1, ba_=ba_: s.activation(
                                sc_ga[:, c0:c1], pb[ba_][:, 0:512], AF.Sigmoid,
                                bias=agb[:, b:b + 1]),
                            [f"pb{ba_}", "agb"], ["sc_ga"])
                    dve(f"la{b}", lambda v: v.tensor_scalar(
                        u1[:], sc_ga[:], csoft[:, b:b + 1], -8.0,
                        ALU.mult, ALU.mult),
                        ["sc_ga", "csoft"], ["u1"])
                    if not main:
                        dve(f"Ared{b}", lambda v: v.reduce_sum(
                            Asum[:, b:b + 1], u1[:], axis=mybir.AxisListType.X),
                            ["u1"], ["Asum"])
                    act(f"aexp{b}", lambda s: s.activation(u2[:], u1[:], AF.Exp),
                        ["u1"], ["u2"])
                    act(f"asq{b}", lambda s: s.activation(u3[:], u2[:], AF.Square),
                        ["u2"], ["u3"])
                    dve(f"m2{b}", lambda v: v.tensor_scalar(
                        u1[:], u3[:], -1.0, 1.0, ALU.mult, ALU.add),
                        ["u3"], ["u1"])
                    act(f"sqrt{b}", lambda s: s.activation(u3[:], u1[:], AF.Sqrt),
                        ["u1"], ["u3"])
                    dve(f"gm{b}", lambda v: v.tensor_mul(sc_ga[:], sc_gx[:],
                                                         u3[:]),
                        ["sc_gx", "u3"], ["sc_ga"])
                    dve(f"nx{b}", lambda v: v.tensor_mul(
                        u1[:], slotX[:, 0:2048], sc_ga[:]),
                        ["slotX", "sc_ga"], ["u1"])
                    if main:
                        init = Hprev[:, b:b + 1]
                        rd = ["u2", "u1", "Hprev"]
                    else:
                        init = 0.0
                        rd = ["u2", "u1"]
                    dve(f"tts{b}", lambda v, init=init: v.tensor_tensor_scan(
                        u3[:], u2[:], u1[:], init, ALU.mult, ALU.add),
                        rd, ["u3"])
                    if not main:
                        dve(f"bc{b}", lambda v: v.tensor_copy(
                            Bcar[:, b:b + 1], u3[:, 2047:2048]), ["u3"], ["Bcar"])
                    if debug and b == 0 and not main:
                        for slot, (tile, nm) in enumerate(
                                [(slotX, "slotX"), (sc_gx, "sc_gx"),
                                 (sc_ga, "sc_ga"), (u1, "u1"), (u2, "u2"),
                                 (u3, "u3")]):
                            dma("pl", "dDB", f"dp{slot}",
                                lambda q, slot=slot, tile=tile: q.dma_start(
                                    out=dbg["d_pre"][slot],
                                    in_=tile[:, 0:T]),
                                [nm], [])
                        dma("pl", "dDB", "dp6",
                            lambda q: q.dma_start(
                                out=dbg["d_pre"][6][:, 0:NB],
                                in_=csoft[:]), ["csoft"], [])
                        dma("pl", "dDB", "dp7",
                            lambda q: q.dma_start(
                                out=dbg["d_pre"][7][:, 0:NB],
                                in_=Asum[:]), ["Asum"], [])
                        dma("pl", "dDB", "dp8",
                            lambda q: q.dma_start(
                                out=dbg["d_pre"][7][:, 32:32 + NB],
                                in_=apT[:]), ["apT"], [])
                        dma("pl", "dDB", "dp9",
                            lambda q: q.dma_start(
                                out=dbg["d_pre"][7][:, 64:64 + NB],
                                in_=Aexp[:]), ["Aexp"], [])
                    if main:
                        for g, (c0, c1) in enumerate(TGA):
                            bank = g % 2
                            tc0 = c0 - 3
                            pe(f"y{b}_{g}",
                               [(lambda t, hc=hc, c0=c0, bank=bank: t.matmul(
                                   pb[bank][:, 0:512],
                                   wbf[ybuf][:, hc * 128:(hc + 1) * 128],
                                   big_a[hc][:, c0:c0 + 512],
                                   start=(hc == 0), stop=(hc == HC - 1)))
                                for hc in range(HC)],
                               [f"wbf{ybuf}"] + [f"ba{i}" for i in range(HC)],
                               [f"pb{bank}"])
                            act(f"ye{b}_{g}",
                                lambda s, tc0=tc0, bank=bank: s.activation(
                                    slotX[:, tc0:tc0 + 512], pb[bank][:, 0:512],
                                    AF.Gelu),
                                [f"pb{bank}"], ["slotX"])
                        if b + 1 < NB:
                            xy_wdma(b + 1, 1, 2)
                        dve(f"z{b}", lambda v: v.tensor_mul(
                            big_b[b][:], slotX[:, 0:2048], u3[:]),
                            ["slotX", "u3"], [f"bb{b}"])

                if stop >= 2:
                    xy_wdma(0, 0, 0)
                    for b in range(NB):
                        emit_block(b, main=False)

                if stop < 3:
                    raise StopIteration
                act("Aexp", lambda s: s.activation(Aexp[:], Asum[:], AF.Exp),
                    ["Asum"], ["Aexp"])
                dma("sy", "dCC", "ccA",
                    lambda q: q.dma_start(out=cc_src[0], in_=Aexp[:]),
                    ["Aexp"], ["cc_src"])
                dma("sy", "dCC", "ccB",
                    lambda q: q.dma_start(out=cc_src[1], in_=Bcar[:]),
                    ["Bcar"], ["cc_src"])

                def cc_em(g, waits, inc_fn):
                    run_waits(g, waits)
                    g.collective_compute(
                        "AllGather", ALU.bypass,
                        replica_groups=[list(range(NCORES))],
                        ins=[cc_src], outs=[cc_dst],
                    ).then_inc(SEM["scc"], 1)
                p.step("pl", cc_em, ["cc_src"], ["cc_dst"],
                       mile_inc=("scc", 1), tag="cc")

                dma("sy", "dCC", "ccR",
                    lambda q: q.dma_start(
                        out=ABall[:].rearrange("p (j two f) -> p j two f", two=2, f=NB),
                        in_=cc_dst.rearrange("j two p f -> p j two f")),
                    ["cc_dst"], ["ABall"])

                for j in range(NCORES):
                    aoff = j * 2 * NB
                    boff = j * 2 * NB + NB
                    cur = j % 2
                    prv = (j - 1) % 2
                    if j == 0:
                        dve("H0", lambda v, boff=boff: v.tensor_copy(
                            Hcur[0][:], ABall[:, boff:boff + NB]),
                            ["ABall"], ["Hcur0"])
                    else:
                        dve(f"Hm{j}", lambda v, aoff=aoff, prv=prv: v.tensor_mul(
                            Htmp[:], ABall[:, aoff:aoff + NB], Hcur[prv][:]),
                            ["ABall", f"Hcur{prv}"], ["Htmp"])
                        dve(f"Ha{j}", lambda v, boff=boff, cur=cur: v.tensor_add(
                            Hcur[cur][:], Htmp[:], ABall[:, boff:boff + NB]),
                            ["Htmp", "ABall"], [f"Hcur{cur}"])
                    dve(f"Hs{j}", lambda v, j=j, cur=cur: v.scalar_tensor_tensor(
                        Hp[(j + 1) % 2][:], Hcur[cur][:], maskbc[:, j:j + 1],
                        Hp[j % 2][:], ALU.mult, ALU.add),
                        [f"Hcur{cur}", "maskbc", f"Hp{j % 2}"],
                        [f"Hp{(j + 1) % 2}"])
                dve("Hfin", lambda v: v.tensor_copy(Hprev[:], Hp[NCORES % 2][:]),
                    [f"Hp{NCORES % 2}"], ["Hprev"])

                if debug:
                    dma("sy", "dDB", "dbg_A",
                        lambda q: q.dma_start(out=dbg["d_A"], in_=Aexp[:]),
                        ["Aexp"], [])
                    dma("sy", "dDB", "dbg_B",
                        lambda q: q.dma_start(out=dbg["d_B"], in_=Bcar[:]),
                        ["Bcar"], [])
                    dma("sy", "dDB", "dbg_H",
                        lambda q: q.dma_start(out=dbg["d_H"], in_=Hprev[:]),
                        ["Hprev"], [])

                if stop < 4:
                    raise StopIteration
                xy_wdma(0, 0, 0)
                xy_wdma(0, 1, 2)
                for b in range(NB):
                    emit_block(b, main=True)

                if debug:
                    for b in range(NB):
                        dma("pl", "dDB", f"dbg_z{b}",
                            lambda q, b=b: q.dma_start(
                                out=dbg["d_z"][b * 128:(b + 1) * 128, :],
                                in_=big_b[b][:]), [f"bb{b}"], [])
                if stop < 5:
                    raise StopIteration
                p.barrier()

                def wr_dma(hc, buf):
                    dma("pl", f"dW{buf}", f"wr{hc}",
                        lambda q, hc=hc, buf=buf: q.dma_start(
                            out=wbf[buf][:].rearrange("p (rc h) -> p rc h", h=128),
                            in_=WrB[hc].rearrange("rc p h -> p rc h")),
                        [], [f"wbf{buf}"])

                wr_dma(0, 0)
                for hc in range(HC):
                    buf = hc % 2
                    if hc + 1 < HC:
                        wr_dma(hc + 1, (hc + 1) % 2)
                    for g, (c0, c1) in enumerate(TGT):
                        bank = g % 2
                        pe(f"ao{hc}_{g}",
                           [(lambda t, rc=rc, c0=c0, bank=bank, buf=buf: t.matmul(
                               pb[bank][:, 0:512],
                               wbf[buf][:, rc * 128:(rc + 1) * 128],
                               big_b[rc][:, c0:c0 + 512],
                               start=(rc == 0), stop=(rc == HC - 1)))
                            for rc in range(HC)],
                           [f"wbf{buf}"] + [f"bb{i}" for i in range(HC)],
                           [f"pb{bank}"])
                        act(f"aoe{hc}_{g}",
                            lambda s, c0=c0, bank=bank, hc=hc: s.activation(
                                big_a[hc][:, c0:c0 + 512], pb[bank][:, 0:512],
                                AF.Copy),
                            [f"pb{bank}"], [f"ba{hc}"])

                for g, (c0, c1) in enumerate(TGT):
                    ln_stats_and_bcast(
                        512, [f"ba{hc}" for hc in range(HC)],
                        lambda hc, c0=c0: big_a[hc][:, c0:c0 + 512],
                        f"ln2_{g}_")
                    for hc in range(HC):
                        xq = hc % 4
                        xsl = u1[:, xq * 512:(xq + 1) * 512]
                        dma("sy", f"dXQ{xq}", f"x2_{g}_{hc}",
                            lambda q, hc=hc, c0=c0, xsl=xsl: q.dma_start(
                                out=xsl,
                                in_=xT[hc * 128:(hc + 1) * 128,
                                       3 + c0:3 + c0 + 512]),
                            [], [f"u1q{xq}"])
                        sqt, sqn = [(sq0, "sq0"), (sq1, "sq1")][hc % 2]
                        dve(f"ln2_{g}_t{hc}",
                            lambda v, hc=hc, c0=c0, sqt=sqt: v.tensor_mul(
                                sqt[:], big_a[hc][:, c0:c0 + 512], bc_r[:]),
                            [f"ba{hc}", "bc_r"], [sqn])
                        dve(f"ln2_{g}_u{hc}",
                            lambda v, sqt=sqt: v.tensor_sub(
                                sqt[:], sqt[:], bc_mr[:]),
                            [sqn, "bc_mr"], [sqn])
                        dve(f"ln2_{g}_x{hc}",
                            lambda v, hc=hc, c0=c0, sqt=sqt, xsl=xsl:
                            v.tensor_add(
                                big_b[hc][:, c0:c0 + 512], sqt[:], xsl),
                            [sqn, f"u1q{xq}"], [f"bb{hc}"])

                if debug:
                    for b in range(NB):
                        dma("pl", "dDB", f"dbg_x1{b}",
                            lambda q, b=b: q.dma_start(
                                out=dbg["d_x1"][b * 128:(b + 1) * 128, :],
                                in_=big_b[b][:]), [f"bb{b}"], [])


            try:
                _phases_A()
                _phases_A2()
                _phases_A3()
            except StopIteration:
                pass
            p.barrier()
            flush()

        with ExitStack() as scB:
            B = scB.enter_context
            wg_t = [B(nc.sbuf_tensor(f"wg{i}", [128, 2048], BF16))
                    for i in range(2)]
            wu_t = [B(nc.sbuf_tensor(f"wu{i}", [128, 2048], BF16))
                    for i in range(2)]
            wd_t = [B(nc.sbuf_tensor(f"wd{i}", [128, 2048], BF16))
                    for i in range(2)]
            gs_t = [B(nc.sbuf_tensor(f"gs{i}", [128, 512], BF16))
                    for i in range(2)]
            ao4 = [B(nc.sbuf_tensor(f"ao4_{i}", [128, 512], BF16))
                   for i in range(HC)]
            hs3 = [B(nc.sbuf_tensor(f"hs3_{i}", [128, 512], BF16))
                   for i in range(HC)]
            outst = [B(nc.sbuf_tensor(f"outst{i}", [128, 512], F32))
                     for i in range(2)]

            def _phases_B():
                if stop < 6:
                    raise StopIteration

                def ffn_chunk(c):
                    cc0 = c * 512
                    ln_stats_and_bcast(
                        512, [f"bb{hc}" for hc in range(HC)],
                        lambda hc, cc0=cc0: big_b[hc][:, cc0:cc0 + 512],
                        f"ln3_{c}_")
                    for hc in range(HC):
                        sqt, sqn = [(sq0, "sq0"), (sq1, "sq1")][hc % 2]
                        dve(f"ln3_{c}_t{hc}",
                            lambda v, hc=hc, cc0=cc0, sqt=sqt: v.tensor_mul(
                                sqt[:], big_b[hc][:, cc0:cc0 + 512], bc_r[:]),
                            [f"bb{hc}", "bc_r"], [sqn])
                        dve(f"ln3_{c}_a{hc}",
                            lambda v, hc=hc, sqt=sqt: v.tensor_sub(
                                hs3[hc][:], sqt[:], bc_mr[:]),
                            [sqn, "bc_mr"], [f"hs3_{hc}"])
                    pe(f"s3_{c}",
                       [(lambda t, hc=hc: t.matmul(
                           pb[6][0:1, 0:512], onesK[:], hs3[hc][:],
                           start=(hc == 0), stop=(hc == HC - 1)))
                        for hc in range(HC)],
                       ["onesK"] + [f"hs3_{i}" for i in range(HC)], ["pb6"])
                    dve(f"s3e_{c}", lambda v: v.tensor_copy(
                        s3row[:], pb[6][0:1, 0:512]), ["pb6"], ["s3row"])

                    for ic in range(NIC):
                        par = ic % 2
                        if c == 0:
                            dma("pl", f"dG{par}", f"wgd{c}_{ic}",
                                lambda q, ic=ic, par=par: q.dma_start(
                                    out=wg_t[par][:].rearrange(
                                        "p (hc i) -> p hc i", i=128),
                                    in_=WgB[ic].rearrange("hc p i -> p hc i")),
                                [], [f"wg{par}"])
                            dma("pl", f"dU{par}", f"wud{c}_{ic}",
                                lambda q, ic=ic, par=par: q.dma_start(
                                    out=wu_t[par][:].rearrange(
                                        "p (hc i) -> p hc i", i=128),
                                    in_=WuB[ic].rearrange("hc p i -> p hc i")),
                                [], [f"wu{par}"])
                        else:
                            dma("sy", f"dG{par}", f"wgd{c}_{ic}",
                                lambda q, ic=ic, par=par: q.dma_start(
                                    out=wg_t[par][:].rearrange(
                                        "p (hc i) -> p hc i", i=128),
                                    in_=WgC[ic].rearrange("hc p i -> p hc i")),
                                ["WgC"], [f"wg{par}"])
                            dma("sy", f"dU{par}", f"wud{c}_{ic}",
                                lambda q, ic=ic, par=par: q.dma_start(
                                    out=wu_t[par][:].rearrange(
                                        "p (hc i) -> p hc i", i=128),
                                    in_=WuC[ic].rearrange("hc p i -> p hc i")),
                                ["WuC"], [f"wu{par}"])
                        bg = [0, 2][par]
                        bu = [1, 3][par]
                        pe(f"g{c}_{ic}",
                           [(lambda t, hc=hc, par=par, bg=bg: t.matmul(
                               pb[bg][:, 0:512],
                               wg_t[par][:, hc * 128:(hc + 1) * 128],
                               hs3[hc][:], start=(hc == 0), stop=False))
                            for hc in range(HC)] +
                           [lambda t, bg=bg: t.matmul(
                               pb[bg][:, 0:512], neg128[:], s3row[:],
                               start=False, stop=True)],
                           [f"wg{par}", "neg128", "s3row"] +
                           [f"hs3_{i}" for i in range(HC)], [f"pb{bg}"])
                        act(f"gse{c}_{ic}",
                            lambda s, ic=ic, par=par, bg=bg: s.activation(
                                gs_t[par][:], pb[bg][:, 0:512], AF.Silu,
                                scale=gsc[:, ic:ic + 1]),
                            [f"pb{bg}", "gsc"], [f"gs{par}"])
                        pe(f"u{c}_{ic}",
                           [(lambda t, hc=hc, par=par, bu=bu: t.matmul(
                               pb[bu][:, 0:512],
                               wu_t[par][:, hc * 128:(hc + 1) * 128],
                               hs3[hc][:], start=(hc == 0), stop=False))
                            for hc in range(HC)] +
                           [lambda t, bu=bu: t.matmul(
                               pb[bu][:, 0:512], neg128[:], s3row[:],
                               start=False, stop=True)],
                           [f"wu{par}", "neg128", "s3row"] +
                           [f"hs3_{i}" for i in range(HC)], [f"pb{bu}"])
                        if c == 0:
                            dma("sy", "dCB", f"cbg{ic}",
                                lambda q, ic=ic, par=par: q.dma_start(
                                    out=WgC[ic].rearrange("hc p i -> p hc i"),
                                    in_=wg_t[par][:].rearrange(
                                        "p (hc i) -> p hc i", i=128)),
                                [f"wg{par}"], ["WgC"])
                            dma("sy", "dCB", f"cbu{ic}",
                                lambda q, ic=ic, par=par: q.dma_start(
                                    out=WuC[ic].rearrange("hc p i -> p hc i"),
                                    in_=wu_t[par][:].rearrange(
                                        "p (hc i) -> p hc i", i=128)),
                                [f"wu{par}"], ["WuC"])
                        at, ao_ = ic // 4, (ic % 4) * 512
                        dve(f"h3_{c}_{ic}",
                            lambda v, ic=ic, par=par, bu=bu, at=at, ao_=ao_:
                            v.scalar_tensor_tensor(
                                big_a[at][:, ao_:ao_ + 512], pb[bu][:, 0:512],
                                usc[:, ic:ic + 1], gs_t[par][:],
                                ALU.mult, ALU.mult),
                            [f"pb{bu}", "usc", f"gs{par}"], [f"ba{at}"])

                    if debug and c == 0:
                        for at in range(HC):
                            dma("pl", "dDB", f"dbg_h3{at}",
                                lambda q, at=at: q.dma_start(
                                    out=dbg["d_h3"][at * 512:(at + 1) * 512, :]
                                    .rearrange("(s p) t -> p s t", p=128),
                                    in_=big_a[at][:, 0:2048].rearrange(
                                        "p (s t) -> p s t", t=512)),
                                [f"ba{at}"], [])

                    pe(f"s3h_{c}",
                       [(lambda t, ic=ic: t.matmul(
                           pb[6][0:1, 0:512], onesK[:],
                           big_a[ic // 4][:, (ic % 4) * 512:(ic % 4) * 512 + 512],
                           start=(ic == 0), stop=(ic == NIC - 1)))
                        for ic in range(NIC)],
                       ["onesK"] + [f"ba{i}" for i in range(HC)], ["pb6"])
                    dve(f"s3he_{c}", lambda v: v.tensor_copy(
                        s3hrow[:], pb[6][0:1, 0:512]), ["pb6"], ["s3hrow"])

                    for hc in range(HC):
                        dbank = 4 + 3 * (hc % 2)
                        for q4 in range(4):
                            par = q4 % 2
                            if c == 0:
                                dma("pl", f"dD{par}", f"wdd{c}_{hc}_{q4}",
                                    lambda q, hc=hc, q4=q4, par=par:
                                    q.dma_start(
                                        out=wd_t[par][:].rearrange(
                                            "p (ic h) -> p ic h", h=128),
                                        in_=WdB[hc, q4 * 16:(q4 + 1) * 16]
                                        .rearrange("ic p h -> p ic h")),
                                    [], [f"wd{par}"])
                            else:
                                dma("sy", f"dD{par}", f"wdd{c}_{hc}_{q4}",
                                    lambda q, hc=hc, q4=q4, par=par:
                                    q.dma_start(
                                        out=wd_t[par][:].rearrange(
                                            "p (ic h) -> p ic h", h=128),
                                        in_=WdC[hc, q4 * 16:(q4 + 1) * 16]
                                        .rearrange("ic p h -> p ic h")),
                                    ["WdC"], [f"wd{par}"])
                            fns = []
                            for jj in range(16):
                                j = q4 * 16 + jj
                                at, ao_ = j // 4, (j % 4) * 512
                                fns.append(
                                    lambda t, par=par, jj=jj, at=at, ao_=ao_,
                                    j=j, dbank=dbank: t.matmul(
                                        pb[dbank][:, 0:512],
                                        wd_t[par][:, jj * 128:(jj + 1) * 128],
                                        big_a[at][:, ao_:ao_ + 512],
                                        start=(j == 0), stop=False))
                            if q4 == 3:
                                fns.append(lambda t, dbank=dbank: t.matmul(
                                    pb[dbank][:, 0:512], neg128[:], s3hrow[:],
                                    start=False, stop=True))
                            pe(f"d{c}_{hc}_{q4}", fns,
                               [f"wd{par}", "neg128", "s3hrow"] +
                               [f"ba{i}" for i in range(HC)], [f"pb{dbank}"])
                            if c == 0:
                                dma("sy", "dCB", f"cbd{hc}_{q4}",
                                    lambda q, hc=hc, q4=q4, par=par:
                                    q.dma_start(
                                        out=WdC[hc, q4 * 16:(q4 + 1) * 16]
                                        .rearrange("ic p h -> p ic h"),
                                        in_=wd_t[par][:].rearrange(
                                            "p (ic h) -> p ic h", h=128)),
                                    [f"wd{par}"], ["WdC"])
                        act(f"de{c}_{hc}",
                            lambda s, hc=hc, dbank=dbank: s.activation(
                                ao4[hc][:], pb[dbank][:, 0:512], AF.Copy,
                                scale=dsc[:, hc:hc + 1]),
                            [f"pb{dbank}", "dsc"], [f"ao4_{hc}"])

                    ln_stats_and_bcast(512, [f"ao4_{hc}" for hc in range(HC)],
                                       lambda hc: ao4[hc][:], f"ln4_{c}_")
                    for hc in range(HC):
                        sqt, sqn = [(sq0, "sq0"), (sq1, "sq1")][hc % 2]
                        ob = hc % 2
                        dve(f"ln4_{c}_t{hc}",
                            lambda v, hc=hc, sqt=sqt: v.tensor_mul(
                                sqt[:], ao4[hc][:], bc_r[:]),
                            [f"ao4_{hc}", "bc_r"], [sqn])
                        dve(f"ln4_{c}_u{hc}",
                            lambda v, sqt=sqt: v.tensor_sub(
                                sqt[:], sqt[:], bc_mr[:]),
                            [sqn, "bc_mr"], [sqn])
                        dve(f"ln4_{c}_o{hc}",
                            lambda v, hc=hc, cc0=cc0, sqt=sqt, ob=ob:
                            v.tensor_add(
                                outst[ob][:], sqt[:],
                                big_b[hc][:, cc0:cc0 + 512]),
                            [sqn, f"bb{hc}"], [f"outst{ob}"])
                        dma("sy", f"dO{ob}", f"out{c}_{hc}",
                            lambda q, hc=hc, cc0=cc0, ob=ob: q.dma_start(
                                out=outT[hc * 128:(hc + 1) * 128,
                                         cc0:cc0 + 512],
                                in_=outst[ob][:]),
                            [f"outst{ob}"], [])

                for c in range(4):
                    ffn_chunk(c)


            try:
                _phases_B()
            except StopIteration:
                pass
            for osem in ("dO0", "dO1"):
                tot = p.ctr[osem]
                if tot:
                    p.raw("sy", lambda o, waits, inc_fn, osem=osem, tot=tot:
                          o.wait_ge(SEM[osem], tot), "fin_" + osem)
            p.barrier()
            flush()

    base.close()
    return nc, dbg



_CACHED = {}


def get_nc(debug=False):
    key = bool(debug)
    if key not in _CACHED:
        _CACHED[key] = build_nc(debug)
    return _CACHED[key]


def make_in_maps(L0_attn_x, L0_attn_W_xy, L0_attn_ig_w, L0_attn_ig_b,
                 L0_attn_ag_w, L0_attn_ag_b, L0_attn_a_param, L0_attn_conv_w,
                 L0_attn_conv_b, L0_attn_W_resid, L0_ffn_gate_w,
                 L0_ffn_gate_scale, L0_ffn_up_w, L0_ffn_up_scale,
                 L0_ffn_down_w, L0_ffn_down_scale):
    C = np.ascontiguousarray

    WxyT = C(np.asarray(L0_attn_W_xy).T)
    igwA = C(np.asarray(L0_attn_ig_w).reshape(NB * 128, 128))
    agwA = C(np.asarray(L0_attn_ag_w).reshape(NB * 128, 128))
    igbA = C(np.asarray(L0_attn_ig_b).T)
    agbA = C(np.asarray(L0_attn_ag_b).T)
    apTA = C(np.asarray(L0_attn_a_param).reshape(NB, 128).T)
    cwTA = C(np.asarray(L0_attn_conv_w).T.reshape(NB, 128, 4)
             .transpose(1, 0, 2).reshape(128, NB * 4))
    cbTA = C(np.asarray(L0_attn_conv_b).reshape(NB, 128).T)
    Wr = np.asarray(L0_attn_W_resid)
    WrBA = C(Wr.T.reshape(HC, 128, HC, 128).transpose(2, 0, 1, 3))
    gw = np.asarray(L0_ffn_gate_w)
    WgBA = C(gw.T.reshape(HC, 128, NIC, 128).transpose(2, 0, 1, 3))
    uw = np.asarray(L0_ffn_up_w)
    WuBA = C(uw.T.reshape(HC, 128, NIC, 128).transpose(2, 0, 1, 3))
    dw = np.asarray(L0_ffn_down_w)
    WdBA = C(dw.T.reshape(NIC, 128, HC, 128).transpose(2, 0, 1, 3))
    gscA = C(np.asarray(L0_ffn_gate_scale).reshape(NIC, 128).T)
    uscA = C(np.asarray(L0_ffn_up_scale).reshape(NIC, 128).T)
    dscA = C(np.asarray(L0_ffn_down_scale).reshape(HC, 128).T)

    xf = np.asarray(L0_attn_x).reshape(16384, H).astype(np.float32)
    in_maps = []
    for c in range(NCORES):
        lo = c * T
        sl = np.zeros((WPAD, H), np.float32)
        if c > 0:
            sl[0:3] = xf[lo - 3:lo]
        sl[3:3 + T] = xf[lo:lo + T]
        xTc = C(sl.T)
        m = np.zeros((1, 8), np.float32)
        if c > 0:
            m[0, c - 1] = 1.0
        in_maps.append(dict(
            xT=xTc, WxyT=WxyT, igw=igwA, agw=agwA, igb=igbA, agb=agbA,
            apT=apTA, cwT=cwTA, cbT=cbTA, WrB=WrBA, WgB=WgBA, WuB=WuBA,
            WdB=WdBA, gscT=gscA, uscT=uscA, dscT=dscA, maskin=m))
    return in_maps


def kernel(**inputs):
    in_maps = make_in_maps(**inputs)
    nc, _ = get_nc(debug=False)
    res = run_bass_kernel_spmd(nc, in_maps, core_ids=list(range(NCORES)))
    outs = [np.asarray(r["outT"]).T for r in res.results]
    return np.concatenate(outs, axis=0).reshape(1, 16384, H).astype(np.float32)


# version 05 (final)
# speedup vs baseline: 1.0294x; 1.0111x over previous
import os
import numpy as np
import ml_dtypes
from contextlib import ExitStack

STEP_LIMIT = int(os.environ.get("KSTEPS", "0")) or None

import concourse.bass as bass
import concourse.mybir as mybir
from concourse.bass_utils import run_bass_kernel_spmd

F32 = mybir.dt.float32
BF16 = mybir.dt.bfloat16
I32 = mybir.dt.int32
AF = mybir.ActivationFunctionType
ALU = mybir.AluOpType

NCORES = 8
H = 2048
T = 2048
W = T + 3
WPAD = W + 1
HC = 16
NB = 16
I_FFN = 8192
NIC = 64
EPS = 1e-5
TGX = [(0, 512), (512, 1024), (1024, 1536), (1536, 2048), (2048, 2051)]
TGA = [(3, 515), (515, 1027), (1027, 1539), (1539, 2051)]
TGT = [(0, 512), (512, 1024), (1024, 1536), (1536, 2048)]


class Prog:
    ENGINES = ("pe", "dv", "ac", "pl", "sy")

    def __init__(self):
        self.nsteps = 0
        self.ops = {e: [] for e in self.ENGINES}
        self.ctr = {}
        self.last_write = {}
        self.readers = {}
        self.waited = {e: {} for e in self.ENGINES}
        self.sems = {}
        for e in self.ENGINES:
            self.ctr["c_" + e] = 0
        self.DMASEMS = (["dX", "dS", "dCC", "dDB", "scc"] +
                        [f"dW{i}" for i in range(3)] +
                        [f"dXQ{i}" for i in range(4)] +
                        ["dG0", "dG1", "dU0", "dU1", "dD0", "dD1",
                         "dO0", "dO1", "dCB"])
        for d in self.DMASEMS:
            self.ctr[d] = 0

    def _wait(self, eng, sem, val, wl):
        if val <= 0:
            return
        if sem == "c_" + eng and eng in ("pe", "sy"):
            return
        if self.waited[eng].get(sem, 0) >= val:
            return
        wl.append((sem, val))

    def step(self, eng, emit, reads=(), writes=(), mile_inc=None,
             record_val=None, tag=""):
        self.nsteps += 1
        if STEP_LIMIT is not None and self.nsteps > STEP_LIMIT:
            return None
        waits = []
        for t in reads:
            lw = self.last_write.get(t)
            if lw is not None:
                self._wait(eng, lw[0], lw[1], waits)
        for t in writes:
            lw = self.last_write.get(t)
            if lw is not None:
                self._wait(eng, lw[0], lw[1], waits)
            for r in self.readers.get(t, ()):
                self._wait(eng, r[0], r[1], waits)
        wd = {}
        for s, v in waits:
            wd[s] = max(wd.get(s, 0), v)
        waits = sorted(wd.items())
        for s, v in waits:
            assert v <= self.ctr[s], (
                f"wait-on-future: {tag} eng={eng} sem={s} v={v} cur={self.ctr[s]}")
            self.waited[eng][s] = max(self.waited[eng].get(s, 0), v)

        if mile_inc is None:
            sem, amt = "c_" + eng, 1
        else:
            sem, amt = mile_inc
        self.ctr[sem] += amt
        mile = (sem, record_val if record_val is not None else self.ctr[sem])
        for t in reads:
            self.readers.setdefault(t, []).append(mile)
        for t in writes:
            self.last_write[t] = mile
            self.readers[t] = [mile]
        self.ops[eng].append((emit, waits, sem, amt, tag))
        return mile

    def raw(self, eng, emit, tag=""):
        self.ops[eng].append((emit, [], None, 0, tag))

    def barrier(self, engines=None):
        engines = engines or list(self.ENGINES)
        targets = [("c_" + e, self.ctr["c_" + e]) for e in engines]
        targets += [(d, self.ctr[d]) for d in self.DMASEMS]
        sems = self.sems
        for e in engines:
            wl = [(s, v) for s, v in targets
                  if s != "c_" + e and v > 0
                  and self.waited[e].get(s, 0) < v]
            for s, v in wl:
                self.waited[e][s] = max(self.waited[e].get(s, 0), v)

            def em2(o, waits, inc_fn, wl=wl):
                for s, v in wl:
                    o.wait_ge(sems[s], v)
            self.ops[e].append((em2, wl, None, 0, "barw"))


def build_nc(debug=False, stop=9):
    nc = bass.Bass("TRN2", target_bir_lowering=False)
    p = Prog()

    xT = nc.dram_tensor("xT", [H, WPAD], F32, kind="ExternalInput").ap()
    WxyT = nc.dram_tensor("WxyT", [H, 2 * H], F32, kind="ExternalInput").ap()
    igw = nc.dram_tensor("igw", [NB * 128, 128], F32, kind="ExternalInput").ap()
    agw = nc.dram_tensor("agw", [NB * 128, 128], F32, kind="ExternalInput").ap()
    igbD = nc.dram_tensor("igb", [128, NB], F32, kind="ExternalInput").ap()
    agbD = nc.dram_tensor("agb", [128, NB], F32, kind="ExternalInput").ap()
    apTD = nc.dram_tensor("apT", [128, NB], F32, kind="ExternalInput").ap()
    cwTD = nc.dram_tensor("cwT", [128, NB * 4], F32, kind="ExternalInput").ap()
    cbTD = nc.dram_tensor("cbT", [128, NB], F32, kind="ExternalInput").ap()
    WrB = nc.dram_tensor("WrB", [HC, HC, 128, 128], F32, kind="ExternalInput").ap()
    WgB = nc.dram_tensor("WgB", [NIC, HC, 128, 128], I32, kind="ExternalInput").ap()
    WuB = nc.dram_tensor("WuB", [NIC, HC, 128, 128], I32, kind="ExternalInput").ap()
    WdB = nc.dram_tensor("WdB", [HC, NIC, 128, 128], I32, kind="ExternalInput").ap()
    gscD = nc.dram_tensor("gscT", [128, NIC], F32, kind="ExternalInput").ap()
    uscD = nc.dram_tensor("uscT", [128, NIC], F32, kind="ExternalInput").ap()
    dscD = nc.dram_tensor("dscT", [128, HC], F32, kind="ExternalInput").ap()
    maskD = nc.dram_tensor("maskin", [1, 8], F32, kind="ExternalInput").ap()

    outT = nc.dram_tensor("outT", [H, T], F32, kind="ExternalOutput").ap()

    WgC = nc.dram_tensor("WgC", [NIC, HC, 128, 128], BF16).ap()
    WuC = nc.dram_tensor("WuC", [NIC, HC, 128, 128], BF16).ap()
    WdC = nc.dram_tensor("WdC", [HC, NIC, 128, 128], BF16).ap()
    cc_src = nc.dram_tensor("cc_src", [2, 128, NB], F32).ap()
    cc_dst = nc.dram_tensor("cc_dst", [NCORES, 2, 128, NB], F32,
                            addr_space="Shared").ap()

    dbg = {}
    if debug:
        for nm, shp in [("d_hs", [H, WPAD]), ("d_z", [H, T]),
                        ("d_A", [128, NB]), ("d_B", [128, NB]),
                        ("d_H", [128, NB]), ("d_x1", [H, T]),
                        ("d_h3", [I_FFN, 512]), ("d_pre", [8, 128, T])]:
            dbg[nm] = nc.dram_tensor(nm, shp, F32, kind="ExternalOutput").ap()

    base = ExitStack()
    E = base.enter_context

    big_a = [E(nc.sbuf_tensor(f"ba{i}", [128, 2056], BF16)) for i in range(HC)]
    big_b = [E(nc.sbuf_tensor(f"bb{i}", [128, 2048], BF16)) for i in range(HC)]
    ones1 = E(nc.sbuf_tensor("ones1", [1, 128], BF16))
    onesK = E(nc.sbuf_tensor("onesK", [128, 1], BF16))
    neg128 = E(nc.sbuf_tensor("neg128", [1, 128], BF16))
    gsc = E(nc.sbuf_tensor("gsc_s", [128, NIC], F32))
    usc = E(nc.sbuf_tensor("usc_s", [128, NIC], F32))
    dsc = E(nc.sbuf_tensor("dsc_s", [128, HC], F32))
    st_m = E(nc.sbuf_tensor("st_m", [1, 512], F32))
    st_q = E(nc.sbuf_tensor("st_q", [1, 512], F32))
    st_v = E(nc.sbuf_tensor("st_v", [1, 512], F32))
    st_v2 = E(nc.sbuf_tensor("st_v2", [1, 512], F32))
    st_rb = E(nc.sbuf_tensor("st_rb", [1, 512], BF16))
    st_mrb = E(nc.sbuf_tensor("st_mrb", [1, 512], BF16))
    bc_r = E(nc.sbuf_tensor("bc_r", [128, 512], BF16))
    bc_mr = E(nc.sbuf_tensor("bc_mr", [128, 512], BF16))
    s3row = E(nc.sbuf_tensor("s3row", [1, 512], BF16))
    s3hrow = E(nc.sbuf_tensor("s3hrow", [1, 512], BF16))
    sq0 = E(nc.sbuf_tensor("sq0", [128, 512], BF16))
    sq1 = E(nc.sbuf_tensor("sq1", [128, 512], BF16))
    epsT = E(nc.sbuf_tensor("epsT", [128, 1], F32))
    hinv = E(nc.sbuf_tensor("hinv", [128, 1], F32))
    negT = E(nc.sbuf_tensor("negT", [128, 1], F32))

    pb = [E(nc.psum_tensor(f"pb{i}", [128, 512], F32)) for i in range(8)]

    for snm in (["c_pe", "c_dv", "c_ac", "c_pl", "c_sy"] + p.DMASEMS):
        p.sems[snm] = E(nc.semaphore(snm))

    SEM = p.sems

    def run_waits(o, waits):
        for s, v in waits:
            o.wait_ge(SEM[s], v)

    def mk(eng):
        def f(name, fn, reads=(), writes=()):
            def em(o, waits, inc_fn):
                run_waits(o, waits)
                inc_fn(fn(o))
            p.step(eng, em, reads, writes, tag=name)
        return f

    dve, act, pol, syq = mk("dv"), mk("ac"), mk("pl"), mk("sy")

    def pe(name, fns, reads=(), writes=()):
        def em(t, waits, inc_fn):
            run_waits(t, waits)
            last = None
            for f in fns:
                last = f(t)
            inc_fn(last)
        p.step("pe", em, reads, writes, tag=name)

    def dma(queue, sem_name, name, fn, reads=(), writes=(), record=None):
        def em(q, waits, inc_fn):
            run_waits(q, waits)
            fn(q).then_inc(SEM[sem_name], 16)
        p.step(queue, em, reads, writes, mile_inc=(sem_name, 16),
               record_val=record, tag=name)

    import os as _os
    _s2b = int(_os.environ.get("S2BANK", "6"))

    def ln_stats_and_bcast(n, src_tiles, src_slices, tagp):
        pe(tagp + "s1",
           [(lambda t, hc=hc: t.matmul(pb[5][0:1, 0:n], onesK[:],
                                       src_slices(hc), start=(hc == 0),
                                       stop=(hc == HC - 1)))
            for hc in range(HC)],
           ["onesK"] + list(src_tiles), ["pb5"])
        for hc in range(HC):
            sqt = [sq0, sq1][hc % 2]
            sqn = ["sq0", "sq1"][hc % 2]
            act(tagp + f"sq{hc}",
                lambda s, hc=hc, sqt=sqt: s.activation(
                    sqt[:, 0:n], src_slices(hc), AF.Square),
                [src_tiles[hc]], [sqn])
            pe(tagp + f"s2_{hc}",
               [lambda t, hc=hc, sqt=sqt: t.matmul(
                   pb[_s2b][0:1, 0:n], onesK[:], sqt[:, 0:n],
                   start=(hc == 0), stop=(hc == HC - 1))],
               ["onesK", sqn], [f"pb{_s2b}"])
        dve(tagp + "m", lambda v: v.tensor_scalar_mul(
            st_m[:, 0:n], pb[5][0:1, 0:n], 1.0 / H), ["pb5"], ["st_m"])
        import os as _os3
        _qm = _os3.environ.get("QMODE", "ts")
        if _qm == "copy":
            dve(tagp + "q", lambda v: v.tensor_copy(
                st_q[:, 0:n], pb[_s2b][0:1, 0:n]),
                [f"pb{_s2b}"], ["st_q"])
        elif _qm == "stv":
            dve(tagp + "q", lambda v: v.tensor_scalar_mul(
                st_v[:, 0:n], pb[_s2b][0:1, 0:n], 1.0 / H),
                [f"pb{_s2b}"], ["st_v"])
        else:
            dve(tagp + "q", lambda v: v.tensor_scalar_mul(
                st_q[:, 0:n], pb[_s2b][0:1, 0:n], 1.0 / H),
                [f"pb{_s2b}"], ["st_q"])
        dve(tagp + "v", lambda v: v.tensor_mul(
            st_v[:, 0:n], st_m[:, 0:n], st_m[:, 0:n]), ["st_m"], ["st_v"])
        dve(tagp + "v2", lambda v: v.tensor_sub(
            st_v2[:, 0:n], st_q[:, 0:n], st_v[:, 0:n]),
            ["st_q", "st_v"], ["st_v2"])
        act(tagp + "sd", lambda s: s.activation(
            st_v[:, 0:n], st_v2[:, 0:n], AF.Sqrt, bias=epsT[0:1, 0:1]),
            ["st_v2", "epsT"], ["st_v"])
        dve(tagp + "r", lambda v: v.reciprocal(st_v2[:, 0:n], st_v[:, 0:n]),
            ["st_v"], ["st_v2"])
        dve(tagp + "rb", lambda v: v.tensor_copy(
            st_rb[:, 0:n], st_v2[:, 0:n]), ["st_v2"], ["st_rb"])
        dve(tagp + "mrb", lambda v: v.tensor_mul(
            st_mrb[:, 0:n], st_m[:, 0:n], st_rb[:, 0:n]),
            ["st_m", "st_rb"], ["st_mrb"])
        pe(tagp + "bc1",
           [lambda t: t.matmul(pb[7][:, 0:n], ones1[:], st_rb[:, 0:n],
                               start=True, stop=True)],
           ["ones1", "st_rb"], ["pb7"])
        dve(tagp + "bcr", lambda v: v.tensor_copy(bc_r[:, 0:n], pb[7][:, 0:n]),
            ["pb7"], ["bc_r"])
        pe(tagp + "bc2",
           [lambda t: t.matmul(pb[7][:, 0:n], ones1[:], st_mrb[:, 0:n],
                               start=True, stop=True)],
           ["ones1", "st_mrb"], ["pb7"])
        dve(tagp + "bcmr", lambda v: v.tensor_copy(
            bc_mr[:, 0:n], pb[7][:, 0:n]), ["pb7"], ["bc_mr"])

    with nc.Block() as block:
        def flush():
            segs = {e: list(p.ops[e]) for e in p.ENGINES}
            for e in p.ENGINES:
                p.ops[e] = []

            def runner(eng_name):
                def run(o):
                    for emit, waits, sem, amt, tag in segs[eng_name]:
                        def inc_fn(instr, sem=sem, amt=amt):
                            if sem is not None:
                                instr.then_inc(SEM[sem], amt)
                        emit(o, waits, inc_fn)
                return run

            block.tensor(runner("pe"))
            block.vector(runner("dv"))
            block.scalar(runner("ac"))
            block.gpsimd(runner("pl"))
            block.sync(runner("sy"))

        with ExitStack() as scA:
            A = scA.enter_context
            igb = A(nc.sbuf_tensor("igb_s", [128, NB], F32))
            agb = A(nc.sbuf_tensor("agb_s", [128, NB], F32))
            apT = A(nc.sbuf_tensor("apT_s", [128, NB], F32))
            csoft = A(nc.sbuf_tensor("csoft", [128, NB], F32))
            cw = A(nc.sbuf_tensor("cw_s", [128, NB * 4], F32))
            cb = A(nc.sbuf_tensor("cb_s", [128, NB], F32))
            mask = A(nc.sbuf_tensor("mask_s", [1, 8], F32))
            maskb = A(nc.sbuf_tensor("mask_bf", [1, 8], BF16))
            maskbc = A(nc.sbuf_tensor("maskbc", [128, 8], F32))
            Asum = A(nc.sbuf_tensor("Asum", [128, NB], F32))
            Aexp = A(nc.sbuf_tensor("Aexp", [128, NB], F32))
            Bcar = A(nc.sbuf_tensor("Bcar", [128, NB], F32))
            ABall = A(nc.sbuf_tensor("ABall", [128, 2 * NB * NCORES], F32))
            Hp = [A(nc.sbuf_tensor(f"Hp{i}", [128, NB], F32))
                  for i in range(2)]
            Hcur = [A(nc.sbuf_tensor(f"Hc{i}", [128, NB], F32))
                    for i in range(2)]
            Htmp = A(nc.sbuf_tensor("Htmp", [128, NB], F32))
            Hprev = A(nc.sbuf_tensor("Hprev", [128, NB], F32))
            igw_b = A(nc.sbuf_tensor("igw_b", [128, NB * 128], BF16))
            agw_b = A(nc.sbuf_tensor("agw_b", [128, NB * 128], BF16))
            slotX = A(nc.sbuf_tensor("slotX", [128, 2056], BF16))
            sc_gx = A(nc.sbuf_tensor("sc_gx", [128, 2048], BF16))
            sc_ga = A(nc.sbuf_tensor("sc_ga", [128, 2048], BF16))
            import os as _os2
            _shrink = bool(_os2.environ.get("SHRINK"))
            u1 = A(nc.sbuf_tensor("u1", [128, 2048], F32))
            u2 = A(nc.sbuf_tensor("u2", [128, 2048], F32))
            u3 = A(nc.sbuf_tensor("u3", [128, 256 if _shrink else 2048], F32))
            wbf = [A(nc.sbuf_tensor(
                f"wbf{i}", [128, 256 if _shrink else 2048], BF16))
                   for i in range(3)]

            def _phases_A():
                import os as _os
                SKIP0 = set(_os.environ.get("SKIP0", ""))
                dve("c_ones1", lambda v: v.memset(ones1[:], 1.0), [], ["ones1"])
                dve("c_onesK", lambda v: v.memset(onesK[:], 1.0), [], ["onesK"])
                dve("c_neg", lambda v: v.memset(neg128[:], -128.0), [], ["neg128"])
                dve("c_hp0", lambda v: v.memset(Hp[0][:], 0.0), [], ["Hp0"])
                dve("c_eps", lambda v: v.memset(epsT[:], EPS), [], ["epsT"])
                dve("c_neg1", lambda v: v.memset(negT[:], -1.0), [], ["negT"])
                dve("c_hinv", lambda v: v.memset(hinv[:], 1.0 / H), [],
                    ["hinv"])

                if "a" not in SKIP0:
                    dma("pl", "dS", "ld_igw",
                        lambda q: q.dma_start(
                            out=igw_b[:].rearrange("p (b j) -> p b j", j=128),
                            in_=igw.rearrange("(b p) j -> p b j", p=128)),
                        [], ["igw_b"], record=176)
                    dma("pl", "dS", "ld_agw",
                        lambda q: q.dma_start(
                            out=agw_b[:].rearrange("p (b j) -> p b j", j=128),
                            in_=agw.rearrange("(b p) j -> p b j", p=128)),
                        [], ["agw_b"], record=176)
                for nm, dst, src in [] if "e" in SKIP0 else [("igb", igb, igbD), ("agb", agb, agbD),
                                     ("apT", apT, apTD), ("cw", cw, cwTD),
                                     ("cb", cb, cbTD), ("gsc", gsc, gscD),
                                     ("usc", usc, uscD), ("dsc", dsc, dscD),
                                     ("mask", mask, maskD)]:
                    dma("sy", "dS", "ld_" + nm,
                        lambda q, d=dst, s=src: q.dma_start(out=d[:], in_=s),
                        [], [nm], record=176)

                if "c" not in SKIP0:
                    _sp()

            def _sp():
                act("sp_abs", lambda s: s.activation(Aexp[:], apT[:], AF.Abs),
                    ["apT"], ["Aexp"])
                act("sp_exp", lambda s: s.activation(Asum[:], Aexp[:], AF.Exp,
                                                     scale=negT[:, 0:1]),
                    ["Aexp", "negT"], ["Asum"])
                act("sp_ln", lambda s: s.activation(Aexp[:], Asum[:], AF.Ln,
                                                    bias=1.0),
                    ["Asum"], ["Aexp"])
                dve("sp_max", lambda v: v.tensor_scalar_max(Asum[:], apT[:], 0.0),
                    ["apT"], ["Asum"])
                dve("csoft", lambda v: v.tensor_add(csoft[:], Asum[:], Aexp[:]),
                    ["Aexp", "Asum"], ["csoft"])

            def _phases_A2():
                import os as _os
                SKIP0 = set(_os.environ.get("SKIP0", ""))
                if "d" not in SKIP0:
                    _maskmm()

            def _maskmm():
                dve("mask_bf", lambda v: v.tensor_copy(maskb[:], mask[:]),
                    ["mask"], ["maskb"])
                pe("mask_mm",
                   [lambda t: t.matmul(pb[7][:, 0:8], ones1[:], maskb[:],
                                       start=True, stop=True)],
                   ["ones1", "maskb"], ["pb7"])
                dve("mask_ev", lambda v: v.tensor_copy(maskbc[:], pb[7][:, 0:8]),
                    ["pb7"], ["maskbc"])

            def _phases_A3():
                if stop < 1:
                    raise StopIteration
                for hc in range(HC):
                    dma("pl", "dX", f"ldx{hc}",
                        lambda q, hc=hc: q.dma_start(
                            out=big_a[hc][:, 0:WPAD],
                            in_=xT[hc * 128:(hc + 1) * 128, :]),
                        [], [f"ba{hc}"], record=256)

                if stop < 1.5:
                    raise StopIteration
                ln1_groups = [(i * 342, (i + 1) * 342) for i in range(6)]
                if stop < 1.7:
                    ln1_groups = ln1_groups[0:1]
                for gi, (c0, c1) in enumerate(ln1_groups):
                    n = c1 - c0
                    names = [f"ba{hc}" for hc in range(HC)]
                    ln_stats_and_bcast(
                        n, names,
                        lambda hc, c0=c0, n=n: big_a[hc][:, c0:c0 + n],
                        f"ln1_{gi}_")
                    for hc in range(HC):
                        sqt = [sq0, sq1][hc % 2]
                        sqn = ["sq0", "sq1"][hc % 2]
                        dve(f"ln1_{gi}_t{hc}",
                            lambda v, hc=hc, c0=c0, n=n, sqt=sqt: v.tensor_mul(
                                sqt[:, 0:n], big_a[hc][:, c0:c0 + n],
                                bc_r[:, 0:n]),
                            [f"ba{hc}", "bc_r"], [sqn])
                        dve(f"ln1_{gi}_a{hc}",
                            lambda v, hc=hc, c0=c0, n=n, sqt=sqt: v.tensor_sub(
                                big_a[hc][:, c0:c0 + n], sqt[:, 0:n],
                                bc_mr[:, 0:n]),
                            [sqn, "bc_mr"], [f"ba{hc}"])

                if debug:
                    for hc in range(HC):
                        dma("pl", "dDB", f"dbg_hs{hc}",
                            lambda q, hc=hc: q.dma_start(
                                out=dbg["d_hs"][hc * 128:(hc + 1) * 128, :],
                                in_=big_a[hc][:, 0:WPAD]), [f"ba{hc}"], [])

                def xy_wdma(b, half, buf):
                    col0 = half * H + b * 128
                    dma("pl", f"dW{buf}", f"wxy{half}_{b}",
                        lambda q: q.dma_start(
                            out=wbf[buf][:].rearrange("p (hc r) -> p hc r", r=128),
                            in_=WxyT[:, col0:col0 + 128].rearrange(
                                "(hc p) r -> p hc r", p=128)),
                        [], [f"wbf{buf}"])

                def emit_block(b, main):
                    xrbuf = b % 2
                    ybuf = 2
                    if b + 1 < NB:
                        xy_wdma(b + 1, 0, (b + 1) % 2)
                    for g, (c0, c1) in enumerate(TGX):
                        n = c1 - c0
                        bank = 2 if g == 4 else g % 2
                        pe(f"xr{b}_{g}",
                           [(lambda t, hc=hc, c0=c0, n=n, bank=bank: t.matmul(
                               pb[bank][:, 0:n],
                               wbf[xrbuf][:, hc * 128:(hc + 1) * 128],
                               big_a[hc][:, c0:c0 + n],
                               start=(hc == 0), stop=(hc == HC - 1)))
                            for hc in range(HC)],
                           [f"wbf{xrbuf}"] + [f"ba{i}" for i in range(HC)],
                           [f"pb{bank}"])
                        dve(f"xre{b}_{g}",
                            lambda v, c0=c0, n=n, bank=bank: v.tensor_copy(
                                slotX[:, c0:c0 + n], pb[bank][:, 0:n]),
                            [f"pb{bank}"], ["slotX"])
                    YB = [0, 1, 2, 7]
                    if main:
                        for g, (c0, c1) in enumerate(TGA):
                            bank = YB[g]
                            pe(f"y{b}_{g}",
                               [(lambda t, hc=hc, c0=c0, bank=bank: t.matmul(
                                   pb[bank][:, 0:512],
                                   wbf[ybuf][:, hc * 128:(hc + 1) * 128],
                                   big_a[hc][:, c0:c0 + 512],
                                   start=(hc == 0), stop=(hc == HC - 1)))
                                for hc in range(HC)],
                               [f"wbf{ybuf}"] + [f"ba{i}" for i in range(HC)],
                               [f"pb{bank}"])
                    dve(f"cv{b}_0",
                        lambda v: v.tensor_scalar(
                            u2[:], slotX[:, 0:2048], cw[:, b * 4:b * 4 + 1],
                            cb[:, b:b + 1], ALU.mult, ALU.add),
                        ["slotX", "cw", "cb"], ["u2"])
                    dve(f"cv{b}_1",
                        lambda v: v.scalar_tensor_tensor(
                            u1[:], slotX[:, 1:2049], cw[:, b * 4 + 1:b * 4 + 2],
                            u2[:], ALU.mult, ALU.add),
                        ["slotX", "cw", "u2"], ["u1"])
                    dve(f"cv{b}_2",
                        lambda v: v.scalar_tensor_tensor(
                            u2[:], slotX[:, 2:2050], cw[:, b * 4 + 2:b * 4 + 3],
                            u1[:], ALU.mult, ALU.add),
                        ["slotX", "cw", "u1"], ["u2"])
                    dve(f"cv{b}_3",
                        lambda v: v.scalar_tensor_tensor(
                            u1[:], slotX[:, 3:2051], cw[:, b * 4 + 3:b * 4 + 4],
                            u2[:], ALU.mult, ALU.add),
                        ["slotX", "cw", "u2"], ["u1"])
                    pol(f"xcb{b}", lambda g: g.tensor_copy(slotX[:, 0:2048],
                                                           u1[:]),
                        ["u1"], ["slotX"])
                    for g, (c0, c1) in enumerate(TGT):
                        bx = 3 if g % 2 == 0 else 5
                        ba_ = 4 if g % 2 == 0 else 6
                        pe(f"xg{b}_{g}",
                           [lambda t, c0=c0, c1=c1, bx=bx: t.matmul(
                               pb[bx][:, 0:512],
                               igw_b[:, b * 128:(b + 1) * 128],
                               slotX[:, c0:c1], start=True, stop=True)],
                           ["igw_b", "slotX"], [f"pb{bx}"])
                        act(f"xge{b}_{g}",
                            lambda s, c0=c0, c1=c1, bx=bx: s.activation(
                                sc_gx[:, c0:c1], pb[bx][:, 0:512], AF.Sigmoid,
                                bias=igb[:, b:b + 1]),
                            [f"pb{bx}", "igb"], ["sc_gx"])
                        pe(f"ag{b}_{g}",
                           [lambda t, c0=c0, c1=c1, ba_=ba_: t.matmul(
                               pb[ba_][:, 0:512],
                               agw_b[:, b * 128:(b + 1) * 128],
                               slotX[:, c0:c1], start=True, stop=True)],
                           ["agw_b", "slotX"], [f"pb{ba_}"])
                        act(f"age{b}_{g}",
                            lambda s, c0=c0, c1=c1, ba_=ba_: s.activation(
                                sc_ga[:, c0:c1], pb[ba_][:, 0:512], AF.Sigmoid,
                                bias=agb[:, b:b + 1]),
                            [f"pb{ba_}", "agb"], ["sc_ga"])
                    dve(f"la{b}", lambda v: v.tensor_scalar(
                        u1[:], sc_ga[:], csoft[:, b:b + 1], -8.0,
                        ALU.mult, ALU.mult),
                        ["sc_ga", "csoft"], ["u1"])
                    if not main:
                        dve(f"Ared{b}", lambda v: v.reduce_sum(
                            Asum[:, b:b + 1], u1[:], axis=mybir.AxisListType.X),
                            ["u1"], ["Asum"])
                    act(f"aexp{b}", lambda s: s.activation(u2[:], u1[:], AF.Exp),
                        ["u1"], ["u2"])
                    act(f"asq{b}", lambda s: s.activation(u3[:], u2[:], AF.Square),
                        ["u2"], ["u3"])
                    dve(f"m2{b}", lambda v: v.tensor_scalar(
                        u1[:], u3[:], -1.0, 1.0, ALU.mult, ALU.add),
                        ["u3"], ["u1"])
                    act(f"sqrt{b}", lambda s: s.activation(u3[:], u1[:], AF.Sqrt),
                        ["u1"], ["u3"])
                    dve(f"gm{b}", lambda v: v.tensor_mul(sc_ga[:], sc_gx[:],
                                                         u3[:]),
                        ["sc_gx", "u3"], ["sc_ga"])
                    dve(f"nx{b}", lambda v: v.tensor_mul(
                        u1[:], slotX[:, 0:2048], sc_ga[:]),
                        ["slotX", "sc_ga"], ["u1"])
                    if main:
                        init = Hprev[:, b:b + 1]
                        rd = ["u2", "u1", "Hprev"]
                    else:
                        init = 0.0
                        rd = ["u2", "u1"]
                    dve(f"tts{b}", lambda v, init=init: v.tensor_tensor_scan(
                        u3[:], u2[:], u1[:], init, ALU.mult, ALU.add),
                        rd, ["u3"])
                    if not main:
                        dve(f"bc{b}", lambda v: v.tensor_copy(
                            Bcar[:, b:b + 1], u3[:, 2047:2048]), ["u3"], ["Bcar"])
                    if debug and b == 0 and not main:
                        for slot, (tile, nm) in enumerate(
                                [(slotX, "slotX"), (sc_gx, "sc_gx"),
                                 (sc_ga, "sc_ga"), (u1, "u1"), (u2, "u2"),
                                 (u3, "u3")]):
                            dma("pl", "dDB", f"dp{slot}",
                                lambda q, slot=slot, tile=tile: q.dma_start(
                                    out=dbg["d_pre"][slot],
                                    in_=tile[:, 0:T]),
                                [nm], [])
                        dma("pl", "dDB", "dp6",
                            lambda q: q.dma_start(
                                out=dbg["d_pre"][6][:, 0:NB],
                                in_=csoft[:]), ["csoft"], [])
                        dma("pl", "dDB", "dp7",
                            lambda q: q.dma_start(
                                out=dbg["d_pre"][7][:, 0:NB],
                                in_=Asum[:]), ["Asum"], [])
                        dma("pl", "dDB", "dp8",
                            lambda q: q.dma_start(
                                out=dbg["d_pre"][7][:, 32:32 + NB],
                                in_=apT[:]), ["apT"], [])
                        dma("pl", "dDB", "dp9",
                            lambda q: q.dma_start(
                                out=dbg["d_pre"][7][:, 64:64 + NB],
                                in_=Aexp[:]), ["Aexp"], [])
                    if main:
                        for g, (c0, c1) in enumerate(TGA):
                            bank = YB[g]
                            act(f"ye{b}_{g}",
                                lambda s, tc0=c0 - 3, bank=bank: s.activation(
                                    slotX[:, tc0:tc0 + 512],
                                    pb[bank][:, 0:512], AF.Gelu),
                                [f"pb{bank}"], ["slotX"])
                        if b + 1 < NB:
                            xy_wdma(b + 1, 1, 2)
                        dve(f"z{b}", lambda v: v.tensor_mul(
                            big_b[b][:], slotX[:, 0:2048], u3[:]),
                            ["slotX", "u3"], [f"bb{b}"])

                if stop >= 2:
                    xy_wdma(0, 0, 0)
                    for b in range(NB):
                        emit_block(b, main=False)

                if stop < 3:
                    raise StopIteration
                act("Aexp", lambda s: s.activation(Aexp[:], Asum[:], AF.Exp),
                    ["Asum"], ["Aexp"])
                dma("sy", "dCC", "ccA",
                    lambda q: q.dma_start(out=cc_src[0], in_=Aexp[:]),
                    ["Aexp"], ["cc_src"])
                dma("sy", "dCC", "ccB",
                    lambda q: q.dma_start(out=cc_src[1], in_=Bcar[:]),
                    ["Bcar"], ["cc_src"])

                def cc_em(g, waits, inc_fn):
                    run_waits(g, waits)
                    g.collective_compute(
                        "AllGather", ALU.bypass,
                        replica_groups=[list(range(NCORES))],
                        ins=[cc_src], outs=[cc_dst],
                    ).then_inc(SEM["scc"], 1)
                p.step("pl", cc_em, ["cc_src"], ["cc_dst"],
                       mile_inc=("scc", 1), tag="cc")

                dma("sy", "dCC", "ccR",
                    lambda q: q.dma_start(
                        out=ABall[:].rearrange("p (j two f) -> p j two f", two=2, f=NB),
                        in_=cc_dst.rearrange("j two p f -> p j two f")),
                    ["cc_dst"], ["ABall"])

                for j in range(NCORES):
                    aoff = j * 2 * NB
                    boff = j * 2 * NB + NB
                    cur = j % 2
                    prv = (j - 1) % 2
                    if j == 0:
                        dve("H0", lambda v, boff=boff: v.tensor_copy(
                            Hcur[0][:], ABall[:, boff:boff + NB]),
                            ["ABall"], ["Hcur0"])
                    else:
                        dve(f"Hm{j}", lambda v, aoff=aoff, prv=prv: v.tensor_mul(
                            Htmp[:], ABall[:, aoff:aoff + NB], Hcur[prv][:]),
                            ["ABall", f"Hcur{prv}"], ["Htmp"])
                        dve(f"Ha{j}", lambda v, boff=boff, cur=cur: v.tensor_add(
                            Hcur[cur][:], Htmp[:], ABall[:, boff:boff + NB]),
                            ["Htmp", "ABall"], [f"Hcur{cur}"])
                    dve(f"Hs{j}", lambda v, j=j, cur=cur: v.scalar_tensor_tensor(
                        Hp[(j + 1) % 2][:], Hcur[cur][:], maskbc[:, j:j + 1],
                        Hp[j % 2][:], ALU.mult, ALU.add),
                        [f"Hcur{cur}", "maskbc", f"Hp{j % 2}"],
                        [f"Hp{(j + 1) % 2}"])
                dve("Hfin", lambda v: v.tensor_copy(Hprev[:], Hp[NCORES % 2][:]),
                    [f"Hp{NCORES % 2}"], ["Hprev"])

                if debug:
                    dma("sy", "dDB", "dbg_A",
                        lambda q: q.dma_start(out=dbg["d_A"], in_=Aexp[:]),
                        ["Aexp"], [])
                    dma("sy", "dDB", "dbg_B",
                        lambda q: q.dma_start(out=dbg["d_B"], in_=Bcar[:]),
                        ["Bcar"], [])
                    dma("sy", "dDB", "dbg_H",
                        lambda q: q.dma_start(out=dbg["d_H"], in_=Hprev[:]),
                        ["Hprev"], [])

                if stop < 4:
                    raise StopIteration
                xy_wdma(0, 0, 0)
                xy_wdma(0, 1, 2)
                for b in range(NB):
                    emit_block(b, main=True)

                if debug:
                    for b in range(NB):
                        dma("pl", "dDB", f"dbg_z{b}",
                            lambda q, b=b: q.dma_start(
                                out=dbg["d_z"][b * 128:(b + 1) * 128, :],
                                in_=big_b[b][:]), [f"bb{b}"], [])
                if stop < 5:
                    raise StopIteration
                p.barrier()

                def wr_dma(hc, buf):
                    dma("pl", f"dW{buf}", f"wr{hc}",
                        lambda q, hc=hc, buf=buf: q.dma_start(
                            out=wbf[buf][:].rearrange("p (rc h) -> p rc h", h=128),
                            in_=WrB[hc].rearrange("rc p h -> p rc h")),
                        [], [f"wbf{buf}"])

                wr_dma(0, 0)
                for hc in range(HC):
                    buf = hc % 2
                    if hc + 1 < HC:
                        wr_dma(hc + 1, (hc + 1) % 2)
                    for g, (c0, c1) in enumerate(TGT):
                        bank = g % 2
                        pe(f"ao{hc}_{g}",
                           [(lambda t, rc=rc, c0=c0, bank=bank, buf=buf: t.matmul(
                               pb[bank][:, 0:512],
                               wbf[buf][:, rc * 128:(rc + 1) * 128],
                               big_b[rc][:, c0:c0 + 512],
                               start=(rc == 0), stop=(rc == HC - 1)))
                            for rc in range(HC)],
                           [f"wbf{buf}"] + [f"bb{i}" for i in range(HC)],
                           [f"pb{bank}"])
                        act(f"aoe{hc}_{g}",
                            lambda s, c0=c0, bank=bank, hc=hc: s.activation(
                                big_a[hc][:, c0:c0 + 512], pb[bank][:, 0:512],
                                AF.Copy),
                            [f"pb{bank}"], [f"ba{hc}"])

                for g, (c0, c1) in enumerate(TGT):
                    ln_stats_and_bcast(
                        512, [f"ba{hc}" for hc in range(HC)],
                        lambda hc, c0=c0: big_a[hc][:, c0:c0 + 512],
                        f"ln2_{g}_")
                    for hc in range(HC):
                        xq = hc % 4
                        xsl = u1[:, xq * 512:(xq + 1) * 512]
                        dma("sy", f"dXQ{xq}", f"x2_{g}_{hc}",
                            lambda q, hc=hc, c0=c0, xsl=xsl: q.dma_start(
                                out=xsl,
                                in_=xT[hc * 128:(hc + 1) * 128,
                                       3 + c0:3 + c0 + 512]),
                            [], [f"u1q{xq}"])
                        sqt, sqn = [(sq0, "sq0"), (sq1, "sq1")][hc % 2]
                        dve(f"ln2_{g}_t{hc}",
                            lambda v, hc=hc, c0=c0, sqt=sqt: v.tensor_mul(
                                sqt[:], big_a[hc][:, c0:c0 + 512], bc_r[:]),
                            [f"ba{hc}", "bc_r"], [sqn])
                        dve(f"ln2_{g}_u{hc}",
                            lambda v, sqt=sqt: v.tensor_sub(
                                sqt[:], sqt[:], bc_mr[:]),
                            [sqn, "bc_mr"], [sqn])
                        dve(f"ln2_{g}_x{hc}",
                            lambda v, hc=hc, c0=c0, sqt=sqt, xsl=xsl:
                            v.tensor_add(
                                big_b[hc][:, c0:c0 + 512], sqt[:], xsl),
                            [sqn, f"u1q{xq}"], [f"bb{hc}"])

                if debug:
                    for b in range(NB):
                        dma("pl", "dDB", f"dbg_x1{b}",
                            lambda q, b=b: q.dma_start(
                                out=dbg["d_x1"][b * 128:(b + 1) * 128, :],
                                in_=big_b[b][:]), [f"bb{b}"], [])


            try:
                _phases_A()
                _phases_A2()
                _phases_A3()
            except StopIteration:
                pass
            p.barrier()
            flush()

        with ExitStack() as scB:
            B = scB.enter_context
            wg_t = [B(nc.sbuf_tensor(f"wg{i}", [128, 2048], BF16))
                    for i in range(2)]
            wu_t = [B(nc.sbuf_tensor(f"wu{i}", [128, 2048], BF16))
                    for i in range(2)]
            wd_t = [B(nc.sbuf_tensor(f"wd{i}", [128, 2048], BF16))
                    for i in range(2)]
            gs_t = [B(nc.sbuf_tensor(f"gs{i}", [128, 512], BF16))
                    for i in range(2)]
            ao4 = [B(nc.sbuf_tensor(f"ao4_{i}", [128, 512], BF16))
                   for i in range(HC)]
            hs3 = [B(nc.sbuf_tensor(f"hs3_{i}", [128, 512], BF16))
                   for i in range(HC)]
            outst = [B(nc.sbuf_tensor(f"outst{i}", [128, 512], F32))
                     for i in range(2)]

            def _phases_B():
                if stop < 6:
                    raise StopIteration

                def ffn_chunk(c):
                    cc0 = c * 512
                    ln_stats_and_bcast(
                        512, [f"bb{hc}" for hc in range(HC)],
                        lambda hc, cc0=cc0: big_b[hc][:, cc0:cc0 + 512],
                        f"ln3_{c}_")
                    for hc in range(HC):
                        sqt, sqn = [(sq0, "sq0"), (sq1, "sq1")][hc % 2]
                        dve(f"ln3_{c}_t{hc}",
                            lambda v, hc=hc, cc0=cc0, sqt=sqt: v.tensor_mul(
                                sqt[:], big_b[hc][:, cc0:cc0 + 512], bc_r[:]),
                            [f"bb{hc}", "bc_r"], [sqn])
                        dve(f"ln3_{c}_a{hc}",
                            lambda v, hc=hc, sqt=sqt: v.tensor_sub(
                                hs3[hc][:], sqt[:], bc_mr[:]),
                            [sqn, "bc_mr"], [f"hs3_{hc}"])
                    pe(f"s3_{c}",
                       [(lambda t, hc=hc: t.matmul(
                           pb[6][0:1, 0:512], onesK[:], hs3[hc][:],
                           start=(hc == 0), stop=(hc == HC - 1)))
                        for hc in range(HC)],
                       ["onesK"] + [f"hs3_{i}" for i in range(HC)], ["pb6"])
                    dve(f"s3e_{c}", lambda v: v.tensor_copy(
                        s3row[:], pb[6][0:1, 0:512]), ["pb6"], ["s3row"])

                    for ic in range(NIC):
                        par = ic % 2
                        if c == 0:
                            dma("pl", f"dG{par}", f"wgd{c}_{ic}",
                                lambda q, ic=ic, par=par: q.dma_start(
                                    out=wg_t[par][:].rearrange(
                                        "p (hc i) -> p hc i", i=128),
                                    in_=WgB[ic].rearrange("hc p i -> p hc i")),
                                [], [f"wg{par}"])
                            dma("pl", f"dU{par}", f"wud{c}_{ic}",
                                lambda q, ic=ic, par=par: q.dma_start(
                                    out=wu_t[par][:].rearrange(
                                        "p (hc i) -> p hc i", i=128),
                                    in_=WuB[ic].rearrange("hc p i -> p hc i")),
                                [], [f"wu{par}"])
                        else:
                            dma("sy", f"dG{par}", f"wgd{c}_{ic}",
                                lambda q, ic=ic, par=par: q.dma_start(
                                    out=wg_t[par][:].rearrange(
                                        "p (hc i) -> p hc i", i=128),
                                    in_=WgC[ic].rearrange("hc p i -> p hc i")),
                                ["WgC"], [f"wg{par}"])
                            dma("sy", f"dU{par}", f"wud{c}_{ic}",
                                lambda q, ic=ic, par=par: q.dma_start(
                                    out=wu_t[par][:].rearrange(
                                        "p (hc i) -> p hc i", i=128),
                                    in_=WuC[ic].rearrange("hc p i -> p hc i")),
                                ["WuC"], [f"wu{par}"])
                        bg = [0, 2][par]
                        bu = [1, 3][par]
                        pe(f"g{c}_{ic}",
                           [(lambda t, hc=hc, par=par, bg=bg: t.matmul(
                               pb[bg][:, 0:512],
                               wg_t[par][:, hc * 128:(hc + 1) * 128],
                               hs3[hc][:], start=(hc == 0), stop=False))
                            for hc in range(HC)] +
                           [lambda t, bg=bg: t.matmul(
                               pb[bg][:, 0:512], neg128[:], s3row[:],
                               start=False, stop=True)],
                           [f"wg{par}", "neg128", "s3row"] +
                           [f"hs3_{i}" for i in range(HC)], [f"pb{bg}"])
                        act(f"gse{c}_{ic}",
                            lambda s, ic=ic, par=par, bg=bg: s.activation(
                                gs_t[par][:], pb[bg][:, 0:512], AF.Silu,
                                scale=gsc[:, ic:ic + 1]),
                            [f"pb{bg}", "gsc"], [f"gs{par}"])
                        pe(f"u{c}_{ic}",
                           [(lambda t, hc=hc, par=par, bu=bu: t.matmul(
                               pb[bu][:, 0:512],
                               wu_t[par][:, hc * 128:(hc + 1) * 128],
                               hs3[hc][:], start=(hc == 0), stop=False))
                            for hc in range(HC)] +
                           [lambda t, bu=bu: t.matmul(
                               pb[bu][:, 0:512], neg128[:], s3row[:],
                               start=False, stop=True)],
                           [f"wu{par}", "neg128", "s3row"] +
                           [f"hs3_{i}" for i in range(HC)], [f"pb{bu}"])
                        if c == 0:
                            dma("sy", "dCB", f"cbg{ic}",
                                lambda q, ic=ic, par=par: q.dma_start(
                                    out=WgC[ic].rearrange("hc p i -> p hc i"),
                                    in_=wg_t[par][:].rearrange(
                                        "p (hc i) -> p hc i", i=128)),
                                [f"wg{par}"], ["WgC"])
                            dma("sy", "dCB", f"cbu{ic}",
                                lambda q, ic=ic, par=par: q.dma_start(
                                    out=WuC[ic].rearrange("hc p i -> p hc i"),
                                    in_=wu_t[par][:].rearrange(
                                        "p (hc i) -> p hc i", i=128)),
                                [f"wu{par}"], ["WuC"])
                        at, ao_ = ic // 4, (ic % 4) * 512
                        dve(f"h3_{c}_{ic}",
                            lambda v, ic=ic, par=par, bu=bu, at=at, ao_=ao_:
                            v.scalar_tensor_tensor(
                                big_a[at][:, ao_:ao_ + 512], pb[bu][:, 0:512],
                                usc[:, ic:ic + 1], gs_t[par][:],
                                ALU.mult, ALU.mult),
                            [f"pb{bu}", "usc", f"gs{par}"], [f"ba{at}"])

                    if debug and c == 0:
                        for at in range(HC):
                            dma("pl", "dDB", f"dbg_h3{at}",
                                lambda q, at=at: q.dma_start(
                                    out=dbg["d_h3"][at * 512:(at + 1) * 512, :]
                                    .rearrange("(s p) t -> p s t", p=128),
                                    in_=big_a[at][:, 0:2048].rearrange(
                                        "p (s t) -> p s t", t=512)),
                                [f"ba{at}"], [])

                    pe(f"s3h_{c}",
                       [(lambda t, ic=ic: t.matmul(
                           pb[6][0:1, 0:512], onesK[:],
                           big_a[ic // 4][:, (ic % 4) * 512:(ic % 4) * 512 + 512],
                           start=(ic == 0), stop=(ic == NIC - 1)))
                        for ic in range(NIC)],
                       ["onesK"] + [f"ba{i}" for i in range(HC)], ["pb6"])
                    dve(f"s3he_{c}", lambda v: v.tensor_copy(
                        s3hrow[:], pb[6][0:1, 0:512]), ["pb6"], ["s3hrow"])

                    for hc in range(HC):
                        dbank = 4 + 3 * (hc % 2)
                        for q4 in range(4):
                            par = q4 % 2
                            if c == 0:
                                dma("pl", f"dD{par}", f"wdd{c}_{hc}_{q4}",
                                    lambda q, hc=hc, q4=q4, par=par:
                                    q.dma_start(
                                        out=wd_t[par][:].rearrange(
                                            "p (ic h) -> p ic h", h=128),
                                        in_=WdB[hc, q4 * 16:(q4 + 1) * 16]
                                        .rearrange("ic p h -> p ic h")),
                                    [], [f"wd{par}"])
                            else:
                                dma("sy", f"dD{par}", f"wdd{c}_{hc}_{q4}",
                                    lambda q, hc=hc, q4=q4, par=par:
                                    q.dma_start(
                                        out=wd_t[par][:].rearrange(
                                            "p (ic h) -> p ic h", h=128),
                                        in_=WdC[hc, q4 * 16:(q4 + 1) * 16]
                                        .rearrange("ic p h -> p ic h")),
                                    ["WdC"], [f"wd{par}"])
                            fns = []
                            for jj in range(16):
                                j = q4 * 16 + jj
                                at, ao_ = j // 4, (j % 4) * 512
                                fns.append(
                                    lambda t, par=par, jj=jj, at=at, ao_=ao_,
                                    j=j, dbank=dbank: t.matmul(
                                        pb[dbank][:, 0:512],
                                        wd_t[par][:, jj * 128:(jj + 1) * 128],
                                        big_a[at][:, ao_:ao_ + 512],
                                        start=(j == 0), stop=False))
                            if q4 == 3:
                                fns.append(lambda t, dbank=dbank: t.matmul(
                                    pb[dbank][:, 0:512], neg128[:], s3hrow[:],
                                    start=False, stop=True))
                            pe(f"d{c}_{hc}_{q4}", fns,
                               [f"wd{par}", "neg128", "s3hrow"] +
                               [f"ba{i}" for i in range(HC)], [f"pb{dbank}"])
                            if c == 0:
                                dma("sy", "dCB", f"cbd{hc}_{q4}",
                                    lambda q, hc=hc, q4=q4, par=par:
                                    q.dma_start(
                                        out=WdC[hc, q4 * 16:(q4 + 1) * 16]
                                        .rearrange("ic p h -> p ic h"),
                                        in_=wd_t[par][:].rearrange(
                                            "p (ic h) -> p ic h", h=128)),
                                    [f"wd{par}"], ["WdC"])
                        act(f"de{c}_{hc}",
                            lambda s, hc=hc, dbank=dbank: s.activation(
                                ao4[hc][:], pb[dbank][:, 0:512], AF.Copy,
                                scale=dsc[:, hc:hc + 1]),
                            [f"pb{dbank}", "dsc"], [f"ao4_{hc}"])

                    ln_stats_and_bcast(512, [f"ao4_{hc}" for hc in range(HC)],
                                       lambda hc: ao4[hc][:], f"ln4_{c}_")
                    for hc in range(HC):
                        sqt, sqn = [(sq0, "sq0"), (sq1, "sq1")][hc % 2]
                        ob = hc % 2
                        dve(f"ln4_{c}_t{hc}",
                            lambda v, hc=hc, sqt=sqt: v.tensor_mul(
                                sqt[:], ao4[hc][:], bc_r[:]),
                            [f"ao4_{hc}", "bc_r"], [sqn])
                        dve(f"ln4_{c}_u{hc}",
                            lambda v, sqt=sqt: v.tensor_sub(
                                sqt[:], sqt[:], bc_mr[:]),
                            [sqn, "bc_mr"], [sqn])
                        dve(f"ln4_{c}_o{hc}",
                            lambda v, hc=hc, cc0=cc0, sqt=sqt, ob=ob:
                            v.tensor_add(
                                outst[ob][:], sqt[:],
                                big_b[hc][:, cc0:cc0 + 512]),
                            [sqn, f"bb{hc}"], [f"outst{ob}"])
                        dma("sy", f"dO{ob}", f"out{c}_{hc}",
                            lambda q, hc=hc, cc0=cc0, ob=ob: q.dma_start(
                                out=outT[hc * 128:(hc + 1) * 128,
                                         cc0:cc0 + 512],
                                in_=outst[ob][:]),
                            [f"outst{ob}"], [])

                for c in range(4):
                    ffn_chunk(c)


            try:
                _phases_B()
            except StopIteration:
                pass
            for osem in ("dO0", "dO1"):
                tot = p.ctr[osem]
                if tot:
                    p.raw("sy", lambda o, waits, inc_fn, osem=osem, tot=tot:
                          o.wait_ge(SEM[osem], tot), "fin_" + osem)
            p.barrier()
            flush()

    base.close()
    return nc, dbg



_CACHED = {}


def get_nc(debug=False):
    key = bool(debug)
    if key not in _CACHED:
        _CACHED[key] = build_nc(debug)
    return _CACHED[key]


def make_in_maps(L0_attn_x, L0_attn_W_xy, L0_attn_ig_w, L0_attn_ig_b,
                 L0_attn_ag_w, L0_attn_ag_b, L0_attn_a_param, L0_attn_conv_w,
                 L0_attn_conv_b, L0_attn_W_resid, L0_ffn_gate_w,
                 L0_ffn_gate_scale, L0_ffn_up_w, L0_ffn_up_scale,
                 L0_ffn_down_w, L0_ffn_down_scale):
    C = np.ascontiguousarray

    WxyT = C(np.asarray(L0_attn_W_xy).T)
    igwA = C(np.asarray(L0_attn_ig_w).reshape(NB * 128, 128))
    agwA = C(np.asarray(L0_attn_ag_w).reshape(NB * 128, 128))
    igbA = C(np.asarray(L0_attn_ig_b).T)
    agbA = C(np.asarray(L0_attn_ag_b).T)
    apTA = C(np.asarray(L0_attn_a_param).reshape(NB, 128).T)
    cwTA = C(np.asarray(L0_attn_conv_w).T.reshape(NB, 128, 4)
             .transpose(1, 0, 2).reshape(128, NB * 4))
    cbTA = C(np.asarray(L0_attn_conv_b).reshape(NB, 128).T)
    Wr = np.asarray(L0_attn_W_resid)
    WrBA = C(Wr.T.reshape(HC, 128, HC, 128).transpose(2, 0, 1, 3))
    gw = np.asarray(L0_ffn_gate_w)
    WgBA = C(gw.T.reshape(HC, 128, NIC, 128).transpose(2, 0, 1, 3))
    uw = np.asarray(L0_ffn_up_w)
    WuBA = C(uw.T.reshape(HC, 128, NIC, 128).transpose(2, 0, 1, 3))
    dw = np.asarray(L0_ffn_down_w)
    WdBA = C(dw.T.reshape(NIC, 128, HC, 128).transpose(2, 0, 1, 3))
    gscA = C(np.asarray(L0_ffn_gate_scale).reshape(NIC, 128).T)
    uscA = C(np.asarray(L0_ffn_up_scale).reshape(NIC, 128).T)
    dscA = C(np.asarray(L0_ffn_down_scale).reshape(HC, 128).T)

    xf = np.asarray(L0_attn_x).reshape(16384, H).astype(np.float32)
    in_maps = []
    for c in range(NCORES):
        lo = c * T
        sl = np.zeros((WPAD, H), np.float32)
        if c > 0:
            sl[0:3] = xf[lo - 3:lo]
        sl[3:3 + T] = xf[lo:lo + T]
        xTc = C(sl.T)
        m = np.zeros((1, 8), np.float32)
        if c > 0:
            m[0, c - 1] = 1.0
        in_maps.append(dict(
            xT=xTc, WxyT=WxyT, igw=igwA, agw=agwA, igb=igbA, agb=agbA,
            apT=apTA, cwT=cwTA, cbT=cbTA, WrB=WrBA, WgB=WgBA, WuB=WuBA,
            WdB=WdBA, gscT=gscA, uscT=uscA, dscT=dscA, maskin=m))
    return in_maps


def kernel(**inputs):
    in_maps = make_in_maps(**inputs)
    nc, _ = get_nc(debug=False)
    res = run_bass_kernel_spmd(nc, in_maps, core_ids=list(range(NCORES)))
    outs = [np.asarray(r["outT"]).T for r in res.results]
    return np.concatenate(outs, axis=0).reshape(1, 16384, H).astype(np.float32)
